# version 09
# baseline (speedup 1.0000x reference)
import numpy as np
import ml_dtypes
import concourse.bass as bass
import concourse.mybir as mybir
from concourse.bass_utils import run_bass_kernel_spmd
from contextlib import ExitStack

BF = ml_dtypes.bfloat16
F32 = mybir.dt.float32
BF16 = mybir.dt.bfloat16
I32 = mybir.dt.int32
AF = mybir.ActivationFunctionType
OP = mybir.AluOpType

T = 4096
TH = 2048
DM = 2048
NF = 16
NC_CH = 32
SC = 512
NCT = 18
WIN_COLS = 4352
WG = 256
NWG = WIN_COLS // WG
EPS_RMS = 1.1920929e-07
EPS_LN = 1e-5
DH = 8192
SLOT_T1 = NF * 128 * T
DETECT_RACES = False


def _consts():
    j = np.arange(128)
    triext = np.zeros((128, 256), np.float32)
    triext[:, 0:128] = (j[:, None] > j[None, :])
    triext[:, 128:192] = 1.0
    uincl = (j[:, None] <= j[None, :]).astype(np.float32)
    return {
        "id_bf": np.eye(128, dtype=BF),
        "id_f32": np.eye(128, dtype=np.float32),
        "triext": triext.astype(BF),
        "uincl": uincl.astype(BF),
        "ones_col_bf": np.ones((128, 1), dtype=BF),
        "ones_row_f": np.ones((1, 128), np.float32),
    }


def build_nc(upto=9):
    nc = bass.Bass("TRN2", target_bir_lowering=False, debug=False,
                   detect_race_conditions=DETECT_RACES)

    xT_h = nc.dram_tensor("xT", [DM, T], F32, kind="ExternalInput")
    xTh_h = nc.dram_tensor("xTh", [DM, TH], F32, kind="ExternalInput")
    winT_h = nc.dram_tensor("winT", [NWG, 128, 16 * WG], F32, kind="ExternalInput")
    woutT_h = nc.dram_tensor("woutT", [16, 128, 32 * 128], F32, kind="ExternalInput")
    wgT_h = nc.dram_tensor("wgT", [64, 128, 16 * 128], F32, kind="ExternalInput")
    wuT_h = nc.dram_tensor("wuT", [64, 128, 16 * 128], F32, kind="ExternalInput")
    wdT_h = nc.dram_tensor("wdT", [16, 128, 64 * 128], F32, kind="ExternalInput")
    convp_h = nc.dram_tensor("convp", [128, NCT * 5], F32, kind="ExternalInput")
    hcol_h = nc.dram_tensor("hcol", [16, 2], F32, kind="ExternalInput")
    hrow_h = nc.dram_tensor("hrow", [1, 16], F32, kind="ExternalInput")
    par_h = nc.dram_tensor("par", [1, 1], I32, kind="ExternalInput")
    non_h = nc.dram_tensor("nonce", [4, 2], I32, kind="ExternalInput")
    out_h = nc.dram_tensor("outT", [DM, TH], F32, kind="ExternalOutput")

    sp_z = nc.dram_tensor("sp_z", [NF, 128, T], BF16)
    sp_xbc = nc.dram_tensor("sp_xbc", [17, 128, T + 3], BF16)
    sp_r1 = nc.dram_tensor("sp_r1", [NF, 128, TH], F32)
    sp_x1 = nc.dram_tensor("sp_x1", [NF, 128, TH], F32)
    sp_r2 = nc.dram_tensor("sp_r2", [NF, 128, TH], F32)
    sp_ssq = nc.dram_tensor("sp_ssq", [1, T], F32)

    wo_bf = nc.dram_tensor("wo_bf", [16, 128, 32 * 128], BF16)
    shm_t1 = nc.dram_tensor("shm_t1", [2, NF, 128, T], BF16, addr_space="Shared")
    shm_ssq = nc.dram_tensor("shm_ssq", [2, 1, T], F32, addr_space="Shared")
    shm_flag = nc.dram_tensor("shm_flag", [2, 2], I32, addr_space="Shared")

    cst = _consts()
    c_idbf = nc.inline_tensor(cst["id_bf"], name="c_idbf")
    c_idf = nc.inline_tensor(cst["id_f32"], name="c_idf")
    c_tri = nc.inline_tensor(cst["triext"], name="c_tri")
    c_u = nc.inline_tensor(cst["uincl"], name="c_u")
    c_ocb = nc.inline_tensor(cst["ones_col_bf"], name="c_ocb")
    c_orf = nc.inline_tensor(cst["ones_row_f"], name="c_orf")

    with ExitStack() as es:
        E = es.enter_context
        idb = E(nc.sbuf_tensor("idb", [128, 128], BF16))
        idf = E(nc.sbuf_tensor("idf", [128, 128], F32))
        tri = E(nc.sbuf_tensor("tri", [128, 256], BF16))
        uin = E(nc.sbuf_tensor("uin", [128, 128], BF16))
        ocb = E(nc.sbuf_tensor("ocb", [128, 1], BF16))
        orf = E(nc.sbuf_tensor("orf", [1, 128], F32))
        convw = E(nc.sbuf_tensor("convw", [128, NCT * 5], F32))
        hc = E(nc.sbuf_tensor("hc", [16, 2], F32))
        Acol = E(nc.sbuf_tensor("Acol", [16, 1], F32))
        Db = E(nc.sbuf_tensor("Db", [128, 16], F32))
        pf = E(nc.sbuf_tensor("pf", [1, 1], I32))
        noncb = E(nc.sbuf_tensor("noncb", [4, 2], I32))
        flagr = E(nc.sbuf_tensor("flagr", [1, 2], I32))
        zeros16 = E(nc.sbuf_tensor("zeros16", [16, 128], F32))
        p_a = E(nc.psum_tensor("p_a", [128, 512], F32))
        p_b = E(nc.psum_tensor("p_b", [128, 512], F32))

        sem_names = [
            "g_cast", "s_load", "s_w", "s_wfree", "s_mm", "s_ev", "s_st", "s_sf",
            "s_dtr", "v_p1b", "a_p1b", "s_xbc", "v_conv", "a_xc", "s_tp", "v_tpe",
            "v_dmx", "s_E", "a_exp", "v_st2", "s_peh", "v_yev", "a_ysq", "s_qmm",
            "v_t1", "s_t1st", "g_flag", "g_xchg", "s_g", "v_rstd", "a_rstd",
            "v_att", "s_r1", "v_row", "a_row", "s_brc", "v_x1", "s_x1t",
            "g_x1b", "s_fw", "v_h", "a_sg", "s_gu", "v_r2", "s_r2t",
            "a_sq2", "s_qmv", "v_out", "s_out", "p0_pe", "p0_dve",
            "s_zld", "a_sz", "v_cdecb", "s_cdmm", "s_ssqst", "v_carry", "s_xth",
            "s_dw5", "s_dmm", "s_ssq2", "s_pc", "v_row2", "a_row2", "s_br2", "a_sq3", "s_qm3",
            "g_wp",
        ]
        S = {n: E(nc.semaphore(n)) for n in sem_names}

        with nc.Block() as b0:
            @b0.sync
            def _(sy):
                for t_, src in [(idb, c_idbf), (idf, c_idf), (tri, c_tri),
                                (uin, c_u), (ocb, c_ocb), (orf, c_orf),
                                (convw, convp_h), (hc, hcol_h), (pf, par_h),
                                (noncb, non_h)]:
                    sy.dma_start(out=t_[:], in_=src.ap()).then_inc(S["s_load"], 16)
                sy.dma_start(out=Db[0:1, :], in_=hrow_h.ap()).then_inc(S["s_load"], 16)

            @b0.scalar
            def _(a):
                a.wait_ge(S["s_load"], 176)
                a.activation(Acol[:], hc[:, 0:1], AF.Exp)
                a.drain()
                a.sem_inc(S["a_p1b"], 1)

            @b0.tensor
            def _(t):
                t.wait_ge(S["s_load"], 176)
                t.matmul(p_a[:, 0:16], orf[:], Db[0:1, :], start=True,
                         stop=True).then_inc(S["p0_pe"], 1)

            @b0.vector
            def _(v):
                v.wait_ge(S["s_load"], 176)
                v.memset(zeros16[:], 0.0)
                v.wait_ge(S["a_p1b"], 1)
                v.tensor_scalar_mul(Acol[:], Acol[:], -1.0)
                v.wait_ge(S["p0_pe"], 1)
                v.tensor_copy(Db[:], p_a[:, 0:16])
                v.drain()
                v.sem_inc(S["p0_dve"], 1)

        if upto == 0:
            with nc.Block() as bex:
                @bex.sync
                def _(sy):
                    sy.wait_ge(S["p0_dve"], 1)
                    sy.dma_start(out=out_h.ap()[0:128, 0:128],
                                 in_=idf[:]).then_inc(S["s_out"], 16)
                    sy.wait_ge(S["s_out"], 16)
            return nc

        esSC = ExitStack()
        scr16 = esSC.enter_context(nc.sbuf_tensor("scr16", [16, T], F32, side="right"))
        esP1 = ExitStack()
        EP1 = esP1.enter_context
        xbf = EP1(nc.sbuf_tensor("xbf", [128, NF * T], BF16))
        wbuf = [EP1(nc.sbuf_tensor(f"wbuf{i}", [128, 16 * WG], BF16)) for i in range(2)]
        stg = EP1(nc.sbuf_tensor("stg", [128, 4 * 512], BF16))
        stgf = EP1(nc.sbuf_tensor("stgf", [128, 2 * 512], F32))

        tiles = []
        spill_i = 0
        for g in range(NWG):
            for otl in range(2):
                ot = 2 * g + otl
                for tt in range(8):
                    sp = spill_i if ot <= 32 else None
                    if ot <= 32:
                        spill_i += 1
                    tiles.append((g, otl, ot, tt, sp))
        NT1 = len(tiles)
        NSP1 = spill_i

        with nc.Block() as b1:
            @b1.gpsimd
            def _(g_):
                g_.wait_ge(S["s_load"], 176)
                for i in range(4):
                    if i >= 1:
                        g_.wait_ge(S["g_cast"], 16 * i)
                    src = xT_h.ap()[i * 512:(i + 1) * 512, :].rearrange(
                        "(ic p) t -> p ic t", p=128)
                    dst = xbf[:].rearrange("p (ic t) -> p ic t", t=T)[:, i * 4:(i + 1) * 4, :]
                    g_.dma_start(out=dst, in_=src).then_inc(S["g_cast"], 16)
                for g in range(NWG):
                    if g >= 1:
                        g_.wait_ge(S["s_w"], 16 * g)
                    if g >= 2:
                        g_.wait_ge(S["s_mm"], 16 * (g - 1))
                    g_.dma_start(out=wbuf[g % 2][:],
                                 in_=winT_h.ap()[g]).then_inc(S["s_w"], 16)

            @b1.tensor
            def _(t):
                t.wait_ge(S["g_cast"], 64)
                t.wait_ge(S["p0_dve"], 1)
                skip_ev = (upto == 13)
                xv = xbf[:].rearrange("p (ic t) -> p ic t", t=T)
                ti = 0
                for g in range(NWG):
                    t.wait_ge(S["s_w"], 16 * (g + 1))
                    wv = wbuf[g % 2][:].rearrange("p (ic o) -> p ic o", o=WG)
                    for otl in range(2):
                        for tt in range(8):
                            pp = p_a if ti % 2 == 0 else p_b
                            if ti >= 2 and not skip_ev:
                                t.wait_ge(S["s_ev"], ti - 1)
                            for ic in range(16):
                                mm = t.matmul(pp[:, 0:512],
                                              wv[:, ic, otl * 128:(otl + 1) * 128],
                                              xv[:, ic, tt * 512:(tt + 1) * 512],
                                              start=(ic == 0), stop=(ic == 15))
                            mm.then_inc(S["s_mm"], 1)
                            ti += 1

            @b1.vector
            def _(v):
                if upto == 13:
                    return
                v.wait_ge(S["p0_dve"], 1)
                v.sem_inc(S["s_st"], 1)
                for ti, (g, otl, ot, tt, sp) in enumerate(tiles):
                    pp = p_a if ti % 2 == 0 else p_b
                    v.wait_ge(S["s_mm"], ti + 1)
                    if sp is not None:
                        slot = sp % 4
                        if sp >= 4 and upto not in (12, 14, 15, 16, 17):
                            v.wait_ge(S["s_sf"], 16 * (sp - 3))
                        if upto == 16:
                            v.tensor_copy(stgf[:, (sp % 2) * 512:(sp % 2 + 1) * 512],
                                          pp[:, 0:512])
                        elif upto == 17:
                            pass
                        else:
                            v.tensor_copy(stg[:, slot * 512:(slot + 1) * 512],
                                          pp[:, 0:512])
                        if upto != 15:
                            v.drain()
                        v.sem_inc(S["s_st"], 1)
                    elif ot == 33 and upto not in (14, 16, 17):
                        v.tensor_copy(scr16[0:16, tt * 512:(tt + 1) * 512],
                                      pp[0:16, 0:512])
                        v.drain()
                        v.sem_inc(S["s_dtr"], 1)
                    v.sem_inc(S["s_ev"], 1)

            @b1.sync
            def _(sy):
                if upto in (12, 13, 14, 15, 16, 17):
                    return
                sy.wait_ge(S["s_st"], 1)
                for ti, (g, otl, ot, tt, sp) in enumerate(tiles):
                    if sp is None:
                        continue
                    slot = sp % 4
                    sy.wait_ge(S["s_st"], sp + 2)
                    if sp >= 1:
                        sy.wait_ge(S["s_sf"], 16 * sp)
                    src = stg[:, slot * 512:(slot + 1) * 512]
                    if ot <= 15:
                        dst = sp_z.ap()[ot, :, tt * 512:(tt + 1) * 512]
                    else:
                        dst = sp_xbc.ap()[ot - 16, :, 3 + tt * 512:3 + (tt + 1) * 512]
                    sy.dma_start(out=dst, in_=src).then_inc(S["s_sf"], 16)

        esP1.close()

        esDT = ExitStack()
        EDT = esDT.enter_context
        dtq = EDT(nc.sbuf_tensor("dtq", [16, T], F32))
        dAq = EDT(nc.sbuf_tensor("dAq", [16, T], F32))
        wtl = EDT(nc.sbuf_tensor("wtl", [16, T], F32))
        alast = EDT(nc.sbuf_tensor("alast", [16, NC_CH], F32))

        if upto in (12, 13, 14, 15, 16, 17):
            with nc.Block() as bex:
                @bex.sync
                def _(sy):
                    sy.wait_ge(S["s_mm"], NT1)
                    if upto in (12, 14, 15, 16, 17):
                        sy.wait_ge(S["s_ev"], NT1)
                    sy.dma_start(out=out_h.ap()[0:128, 0:128],
                                 in_=idf[:]).then_inc(S["s_out"], 16)
                    sy.wait_ge(S["s_out"], 16)
            esDT.close()
            esSC.close()
            return nc

        if upto == 11:
            with nc.Block() as bex:
                @bex.sync
                def _(sy):
                    sy.wait_ge(S["s_sf"], 16 * NSP1)
                    sy.wait_ge(S["s_dtr"], 8)
                    sy.dma_start(out=out_h.ap()[0:128, 0:128],
                                 in_=idf[:]).then_inc(S["s_out"], 16)
                    sy.wait_ge(S["s_out"], 16)
            esDT.close()
            esSC.close()
            return nc

        with nc.Block() as b1b:
            @b1b.vector
            def _(v):
                v.wait_ge(S["s_dtr"], 8)
                v.tensor_scalar(dtq[:], scr16[0:16, :], hc[:, 1:2], None, OP.add)
                v.drain()
                v.sem_inc(S["v_p1b"], 1)
                v.wait_ge(S["a_p1b"], 4)
                v.tensor_scalar_max(scr16[0:16, :], dtq[:], 0.0)
                v.drain()
                v.tensor_add(dtq[:], scr16[0:16, :], dAq[:])
                v.drain()
                v.tensor_scalar(dAq[:], dtq[:], Acol[:], None, OP.mult)
                v.drain()
                for c in range(NC_CH):
                    v.tensor_tensor_scan(wtl[:, c * 128:(c + 1) * 128],
                                         dAq[:, c * 128:(c + 1) * 128],
                                         zeros16[:], 0.0, OP.add, OP.add)
                v.drain()
                al_src = bass.AP(wtl[:].tensor, wtl[:].offset + 127,
                                 [wtl[:].ap[0], [128, NC_CH]])
                v.tensor_copy(alast[:], al_src)
                v.drain()
                a3 = wtl[:].rearrange("h (c q) -> h c q", q=128)
                alb = bass.AP(alast[:].tensor, alast[:].offset,
                              [alast[:].ap[0], [1, NC_CH], [0, 128]])
                v.tensor_tensor(a3, a3, alb, OP.subtract)
                v.drain()
                v.sem_inc(S["v_p1b"], 1)
                v.wait_ge(S["a_p1b"], 6)
                v.tensor_mul(wtl[:], wtl[:], dtq[:])
                v.drain()
                v.sem_inc(S["v_p1b"], 1)

            @b1b.scalar
            def _(a):
                a.wait_ge(S["v_p1b"], 1)
                a.activation(dAq[:], dtq[:], AF.Abs)
                a.drain()
                a.sem_inc(S["a_p1b"], 1)
                a.activation(dAq[:], dAq[:], AF.Exp, scale=-1.0)
                a.drain()
                a.sem_inc(S["a_p1b"], 1)
                a.activation(dAq[:], dAq[:], AF.Ln, bias=1.0)
                a.drain()
                a.sem_inc(S["a_p1b"], 1)
                a.wait_ge(S["v_p1b"], 2)
                a.activation(wtl[:], wtl[:], AF.Exp, scale=-1.0)
                a.drain()
                a.sem_inc(S["a_p1b"], 1)
                a.activation(alast[:], alast[:], AF.Exp)
                a.drain()
                a.sem_inc(S["a_p1b"], 1)

        esSC.close()

        if upto == 1:
            esDT.close()
            with nc.Block() as bex:
                @bex.sync
                def _(sy):
                    sy.wait_ge(S["s_sf"], 16 * NSP1)
                    sy.wait_ge(S["v_p1b"], 3)
                    sy.dma_start(out=out_h.ap()[0:128, 0:51],
                                 in_=idf[:, 0:51]).then_inc(S["s_out"], 16)
                    sy.wait_ge(S["s_out"], 16)
            return nc

        esP2 = ExitStack()
        EP2 = esP2.enter_context
        carry = EP2(nc.sbuf_tensor("carry", [64, 16 * 128], F32))
        
        xbcs = EP2(nc.sbuf_tensor("xbcs", [128, NCT * (SC + 3)], BF16))
        accs = EP2(nc.sbuf_tensor("accs", [128, NCT * SC], BF16))
        cvt = EP2(nc.sbuf_tensor("cvt", [128, NCT * SC], BF16))
        xcs = [EP2(nc.sbuf_tensor(f"xcs{i}", [128, NCT * SC], BF16)) for i in range(2)]
        zc = [EP2(nc.sbuf_tensor(f"zc{i}", [128, NF * 128], BF16)) for i in range(2)]
        xt_sb = EP2(nc.sbuf_tensor("xt_sb", [128, 2 * 16 * 128], BF16))
        Bt_sb = EP2(nc.sbuf_tensor("Bt_sb", [128, 64], BF16))
        Gm_sb = EP2(nc.sbuf_tensor("Gm_sb", [128, 2 * 128], BF16))
        tch = EP2(nc.sbuf_tensor("tch", [128, 2 * 64], F32))
        cdecb = EP2(nc.sbuf_tensor("cdecb", [128, 2 * 16], F32))
        dmx = EP2(nc.sbuf_tensor("dmx", [128, 4 * 256], BF16))
        expe = EP2(nc.sbuf_tensor("expe", [128, 4 * 256], BF16))
        Stb = EP2(nc.sbuf_tensor("Stb", [128, 4 * 128], BF16))
        Svb = EP2(nc.sbuf_tensor("Svb", [64, 4 * 128], BF16))
        Bddb = EP2(nc.sbuf_tensor("Bddb", [128, 4 * 64], BF16))
        prevb = EP2(nc.sbuf_tensor("prevb", [64, 4 * 128], BF16))
        ybuf = EP2(nc.sbuf_tensor("ybuf", [128, 16 * 128], BF16))
        ysq = EP2(nc.sbuf_tensor("ysq", [128, 16 * 128], BF16))
        t1st = EP2(nc.sbuf_tensor("t1st", [128, 4 * NF * 128], BF16))
        ssqrow = EP2(nc.sbuf_tensor("ssqrow", [1, 128], F32))
        p_tA = EP2(nc.psum_tensor("p_tA", [128, 1024], BF16))
        p_tB = EP2(nc.psum_tensor("p_tB", [128, 1024], BF16))
        p_tC = EP2(nc.psum_tensor("p_tC", [128, 128], BF16))
        p_tD = EP2(nc.psum_tensor("p_tD", [128, 512], F32))
        p_e = [EP2(nc.psum_tensor(f"p_e{i}", [128, 512], F32)) for i in range(2)]
        p_e = p_e + [p_a, p_b]

        def xview(sc):
            return xcs[sc % 2][:].rearrange("p (f t) -> p f t", t=SC)

        with nc.Block() as b2:
            @b2.vector
            def _(v):
                v.wait_ge(S["s_mm"], NT1)
                v.wait_ge(S["s_sf"], 16 * NSP1)
                v.memset(carry[:], 0.0)
                v.drain()
                v.sem_inc(S["v_carry"], 1)

                def dve_dmx(H):
                    c, hl = H // 16, H % 16
                    ck = c % 2
                    st = H % 4
                    if H >= 4:
                        v.wait_ge(S["s_E"], H - 3)
                        v.wait_ge(S["s_peh"], H - 3)
                    v.tensor_scalar(dmx[:, st * 256:(st + 1) * 256], tri[:],
                                    tch[:, ck * 64 + 32 + hl:ck * 64 + 33 + hl],
                                    None, OP.mult)
                    v.tensor_scalar(Bddb[:, st * 64:(st + 1) * 64], Bt_sb[:],
                                    tch[:, ck * 64 + 16 + hl:ck * 64 + 17 + hl],
                                    None, OP.mult)
                    v.tensor_copy(prevb[:, st * 128:(st + 1) * 128],
                                  carry[:, hl * 128:(hl + 1) * 128])
                    v.drain()
                    v.sem_inc(S["v_dmx"], 1)

                def dve_st(H, xv):
                    c, hl = H // 16, H % 16
                    ck = c % 2
                    st = H % 4
                    v.wait_ge(S["a_exp"], H + 1)
                    v.scalar_tensor_tensor(
                        Stb[:, st * 128:(st + 1) * 128],
                        expe[:, st * 256:st * 256 + 128],
                        tch[:, ck * 64 + hl:ck * 64 + hl + 1],
                        Gm_sb[:, ck * 128:(ck + 1) * 128], OP.mult, OP.mult)
                    v.tensor_tensor(
                        Svb[:, st * 128:(st + 1) * 128],
                        xv[0:64, 17, (c % 4) * 128:(c % 4 + 1) * 128],
                        expe[0:64, st * 256 + 128:st * 256 + 256], OP.mult)
                    v.drain()
                    v.sem_inc(S["v_st2"], 1)

                def dve_yev(H, xv):
                    c, hl = H // 16, H % 16
                    ck = c % 2
                    st = H % 4
                    v.wait_ge(S["s_peh"], H + 1)
                    if hl == 0 and c >= 1:
                        v.wait_ge(S["a_ysq"], c)
                    v.scalar_tensor_tensor(
                        ybuf[:, hl * 128:(hl + 1) * 128],
                        xv[:, hl, (c % 4) * 128:(c % 4 + 1) * 128],
                        Db[:, hl:hl + 1], p_e[st][:, 256:384], OP.mult, OP.add)
                    v.scalar_tensor_tensor(
                        carry[:, hl * 128:(hl + 1) * 128],
                        carry[:, hl * 128:(hl + 1) * 128],
                        cdecb[0:64, ck * 16 + hl:ck * 16 + hl + 1],
                        p_e[st][0:64, 384:512], OP.mult, OP.add)
                    v.drain()
                    v.sem_inc(S["v_yev"], 1)

                for c in range(NC_CH):
                    sc, cl, ck = c // 4, c % 4, c % 2
                    if cl == 0:
                        v.wait_ge(S["s_xbc"], 48 * (sc + 1))
                        xb3 = xbcs[:].rearrange("p (f t) -> p f t", t=SC + 3)
                        cw3 = convw[:].rearrange("p (f k) -> p f k", k=5)
                        def cwb(k):
                            s_ = cw3[:, :, k:k + 1]
                            return bass.AP(s_.tensor, s_.offset,
                                           [s_.ap[0], s_.ap[1], [0, SC]])
                        a3 = accs[:].rearrange("p (f t) -> p f t", t=SC)
                        b3_ = cvt[:].rearrange("p (f t) -> p f t", t=SC)
                        if sc == 0:
                            v.memset(xb3[:, :, 0:3], 0)
                            v.drain()
                        v.tensor_tensor(a3, xb3[:, :, 0:SC], cwb(0), OP.mult)
                        v.drain()
                        for k in range(1, 4):
                            v.tensor_tensor(b3_, xb3[:, :, k:k + SC], cwb(k), OP.mult)
                            v.drain()
                            v.tensor_tensor(a3, a3, b3_, OP.add)
                            v.drain()
                        v.tensor_tensor(a3, a3, cwb(4), OP.add)
                        v.drain()
                        v.tensor_copy(xb3[:, :, 0:3], xb3[:, :, SC:SC + 3])
                        v.drain()
                        v.sem_inc(S["v_conv"], 1)
                    xv = xcs[sc % 2][:].rearrange("p (f t) -> p f t", t=SC)
                    v.drain()
                    v.wait_ge(S["s_tp"], c + 1)
                    if c >= 2:
                        v.wait_ge(S["s_peh"], 16 * (c - 1))
                    v.tensor_copy(xt_sb[:, ck * 2048:ck * 2048 + 1024], p_tA[:])
                    v.tensor_copy(xt_sb[:, ck * 2048 + 1024:ck * 2048 + 2048], p_tB[:])
                    v.tensor_copy(Bt_sb[:], p_tC[:, 0:64])
                    v.tensor_copy(tch[:, ck * 64:ck * 64 + 48], p_tD[:, 0:48])
                    v.tensor_copy(tch[0:1, ck * 64 + 48:ck * 64 + 64],
                                  p_tD[0:1, 48:64])
                    v.drain()
                    v.sem_inc(S["v_cdecb"], 1)
                    v.tensor_tensor(Gm_sb[:, ck * 128:(ck + 1) * 128],
                                    p_tD[:, 128:256], uin[:], OP.mult)
                    v.drain()
                    v.sem_inc(S["v_tpe"], 1)
                    v.wait_ge(S["s_cdmm"], c + 1)
                    v.tensor_copy(cdecb[:, ck * 16:(ck + 1) * 16], p_tD[:, 272:288])
                    v.drain()
                    for hl in range(16):
                        H = c * 16 + hl
                        dve_dmx(H)
                        if hl >= 1:
                            dve_st(H - 1, xv)
                        if hl >= 2:
                            dve_yev(H - 2, xv)
                    dve_st(c * 16 + 15, xv)
                    dve_yev(c * 16 + 14, xv)
                    dve_yev(c * 16 + 15, xv)
                    v.wait_ge(S["a_sz"], c + 1)
                    if c >= 4:
                        v.wait_ge(S["s_t1st"], 16 * (c // 4))
                    t1v = t1st[:].rearrange("p (f t) -> p f t", t=512)
                    v.tensor_tensor(t1v[:, :, cl * 128:(cl + 1) * 128],
                                    ybuf[:].rearrange("p (f t) -> p f t", t=128),
                                    zc[c % 2][:].rearrange("p (f t) -> p f t", t=128),
                                    OP.mult)
                    v.drain()
                    v.sem_inc(S["v_t1"], 1)
                    v.wait_ge(S["s_qmm"], c + 1)
                    if c >= 1:
                        v.wait_ge(S["s_ssqst"], 16 * c)
                    v.tensor_copy(ssqrow[:], p_tD[0:1, 384:512])
                    v.drain()
                    v.sem_inc(S["v_carry"], 1)

            @b2.scalar
            def _(a):
                for c in range(NC_CH):
                    sc, cl = c // 4, c % 4
                    if cl == 0:
                        a.wait_ge(S["v_conv"], sc + 1)
                        if sc >= 2:
                            a.wait_ge(S["v_yev"], 64 * (sc - 1))
                        a.activation(xcs[sc % 2][:], accs[:], AF.Silu)
                        a.drain()
                        a.sem_inc(S["a_xc"], 1)
                    a.wait_ge(S["s_zld"], 16 * (c + 1))
                    a.activation(zc[c % 2][:], zc[c % 2][:], AF.Silu)
                    a.drain()
                    a.sem_inc(S["a_sz"], 1)
                    for hl in range(16):
                        H = c * 16 + hl
                        st = H % 4
                        a.wait_ge(S["s_E"], H + 1)
                        if H >= 4:
                            a.wait_ge(S["v_st2"], H - 3)
                        a.activation(expe[:, st * 256:(st + 1) * 256],
                                     p_e[st][:, 0:256], AF.Exp)
                        a.drain()
                        a.sem_inc(S["a_exp"], 1)
                    a.wait_ge(S["v_yev"], 16 * (c + 1))
                    a.activation(ysq[:], ybuf[:], AF.Square)
                    a.drain()
                    a.sem_inc(S["a_ysq"], 1)

            @b2.tensor
            def _(t):
                t.wait_ge(S["v_p1b"], 3)
                t.wait_ge(S["s_ev"], NT1)

                def pe_e(H):
                    st = H % 4
                    t.wait_ge(S["v_dmx"], H + 1)
                    if H >= 4:
                        t.wait_ge(S["a_exp"], H - 3)
                        t.wait_ge(S["v_yev"], H - 3)
                    t.matmul(p_e[st][:, 0:128], dmx[:, st * 256:st * 256 + 128],
                             uin[:], start=True, stop=True)
                    t.matmul(p_e[st][:, 128:256],
                             dmx[:, st * 256 + 128:st * 256 + 256],
                             uin[:], start=True, stop=True).then_inc(S["s_E"], 1)

                def pe_ys(H):
                    c, hl = H // 16, H % 16
                    st = H % 4
                    ck = c % 2
                    t.wait_ge(S["v_st2"], H + 1)
                    t.matmul(p_e[st][:, 256:384],
                             xt_sb[:, ck * 2048 + hl * 128:ck * 2048 + (hl + 1) * 128],
                             Stb[:, st * 128:(st + 1) * 128], start=True, stop=False)
                    t.matmul(p_e[st][:, 256:384],
                             prevb[:, st * 128:(st + 1) * 128],
                             Svb[:, st * 128:(st + 1) * 128], start=False, stop=True)
                    t.matmul(p_e[st][0:64, 384:512],
                             Bddb[:, st * 64:(st + 1) * 64],
                             xt_sb[:, ck * 2048 + hl * 128:ck * 2048 + (hl + 1) * 128],
                             start=True, stop=True).then_inc(S["s_peh"], 1)

                for c in range(NC_CH):
                    sc, cl, ck = c // 4, c % 4, c % 2
                    xv = xcs[sc % 2][:].rearrange("p (f t) -> p f t", t=SC)
                    t.wait_ge(S["a_xc"], sc + 1)
                    t.wait_ge(S["v_carry"], c + 1)
                    if c >= 1:
                        t.wait_ge(S["v_tpe"], c)
                    for hh in range(16):
                        dstp = p_tA if hh < 8 else p_tB
                        t.transpose(dstp[:, (hh % 8) * 128:(hh % 8 + 1) * 128],
                                    xv[:, hh, cl * 128:(cl + 1) * 128], idb[:])
                    t.transpose(p_tC[:, 0:64], xv[0:64, 16, cl * 128:(cl + 1) * 128],
                                idb[0:64, 0:64])
                    t.transpose(p_tD[:, 0:16], dtq[:, c * 128:(c + 1) * 128],
                                idf[0:16, 0:16])
                    t.transpose(p_tD[:, 16:32], wtl[:, c * 128:(c + 1) * 128],
                                idf[0:16, 0:16])
                    t.transpose(p_tD[:, 32:48], dAq[:, c * 128:(c + 1) * 128],
                                idf[0:16, 0:16])
                    t.transpose(p_tD[0:1, 48:64], alast[:, c:c + 1], idf[0:16, 0:16])
                    t.matmul(p_tD[:, 128:256], xv[0:64, 16, cl * 128:(cl + 1) * 128],
                             xv[0:64, 17, cl * 128:(cl + 1) * 128],
                             start=True, stop=True).then_inc(S["s_tp"], 1)
                    t.wait_ge(S["v_cdecb"], c + 1)
                    t.matmul(p_tD[:, 272:288], orf[:],
                             tch[0:1, ck * 64 + 48:ck * 64 + 64],
                             start=True, stop=True).then_inc(S["s_cdmm"], 1)
                    for hl in range(16):
                        H = c * 16 + hl
                        pe_e(H)
                        if hl >= 1:
                            pe_ys(H - 1)
                    pe_ys(c * 16 + 15)
                    t.wait_ge(S["a_ysq"], c + 1)
                    for f in range(16):
                        mm = t.matmul(p_tD[0:1, 384:512], ocb[:],
                                      ysq[:, f * 128:(f + 1) * 128],
                                      start=(f == 0), stop=(f == 15))
                    mm.then_inc(S["s_qmm"], 1)

            @b2.sync
            def _(sy):
                with sy.register("t1offc") as rtc, sy.register("sqoff") as rq:
                    for c in range(NC_CH):
                        sc, cl = c // 4, c % 4
                        if cl == 0:
                            if sc == 0:
                                sy.wait_ge(S["s_mm"], NT1)
                                sy.wait_ge(S["s_sf"], 16 * NSP1)
                                sy.wait_ge(S["v_p1b"], 3)
                            else:
                                sy.wait_ge(S["v_conv"], sc)
                            if sc >= 2:
                                sy.wait_ge(S["v_yev"], 64 * (sc - 1))
                            if sc >= 1:
                                sy.wait_ge(S["s_xbc"], 48 * sc)
                            src = sp_xbc.ap()[:, :, 3 + sc * SC:3 + (sc + 1) * SC] \
                                .rearrange("f p t -> p f t")
                            dst = xbcs[:].rearrange("p (f t) -> p f t",
                                                    t=SC + 3)[:, 0:17, 3:SC + 3]
                            sy.dma_start(out=dst, in_=src).then_inc(S["s_xbc"], 16)
                            dst2 = xbcs[:].rearrange("p (f t) -> p f t",
                                                     t=SC + 3)[0:64, 17, 3:SC + 3]
                            sy.dma_start(
                                out=dst2,
                                in_=sp_xbc.ap()[16, 64:128, 3 + sc * SC:3 + (sc + 1) * SC]
                            ).then_inc(S["s_xbc"], 16)
                            dst3 = xbcs[:].rearrange("p (f t) -> p f t",
                                                     t=SC + 3)[64:128, 17, 3:SC + 3]
                            sy.dma_start(
                                out=dst3,
                                in_=sp_xbc.ap()[16, 0:64, 3 + sc * SC:3 + (sc + 1) * SC]
                            ).then_inc(S["s_xbc"], 16)
                        if c >= 2:
                            sy.wait_ge(S["v_t1"], c - 1)
                        if c >= 1:
                            sy.wait_ge(S["s_zld"], 16 * c)
                        src = sp_z.ap()[:, :, c * 128:(c + 1) * 128] \
                            .rearrange("f p t -> p f t")
                        dst = zc[c % 2][:].rearrange("p (f t) -> p f t", t=128)
                        sy.dma_start(out=dst, in_=src).then_inc(S["s_zld"], 16)
                        sy.wait_ge(S["v_carry"], c + 2)
                        if c >= 1:
                            sy.wait_ge(S["s_ssqst"], 16 * c)
                        sy.dma_start(out=sp_ssq.ap()[0:1, c * 128:(c + 1) * 128],
                                     in_=ssqrow[:]).then_inc(S["s_ssqst"], 16)
                    del rtc, rq

            @b2.gpsimd
            def _(g_):
                for i in range(16):
                    if i >= 1:
                        g_.wait_ge(S["s_pc"], 16 * i)
                    g_.dma_start(out=wo_bf.ap()[i],
                                 in_=woutT_h.ap()[i]).then_inc(S["s_pc"], 16)
                with g_.register("pt_g") as rp2, g_.register("t1o") as rt1, \
                     g_.register("t1oc") as rtc, g_.register("sqo") as rq:
                    g_.wait_ge(S["s_load"], 176)
                    g_.reg_load(rp2, pf[:])
                    g_.reg_alu(rt1, rp2, SLOT_T1, OP.mult)
                    g_.reg_alu(rq, rp2, T, OP.mult)
                    for sc in range(8):
                        g_.wait_ge(S["v_t1"], 4 * (sc + 1))
                        if sc >= 1:
                            g_.wait_ge(S["s_t1st"], 16 * sc)
                        g_.reg_alu(rtc, rt1, sc * 512, OP.add)
                        t1dst = bass.AP(shm_t1, rtc,
                                        [[T, 128], [128 * T, NF], [1, 512]])
                        src = t1st[:].rearrange("p (f t) -> p f t", t=512)
                        g_.dma_start(out=t1dst, in_=src).then_inc(S["s_t1st"], 16)
                    g_.wait_ge(S["s_ssqst"], 16 * NC_CH)
                    sqdst = bass.AP(shm_ssq, rq, [[T, 1], [1, T]])
                    g_.dma_start(out=sqdst, in_=sp_ssq.ap()).then_inc(S["s_ssq2"], 16)
                g_.wait_ge(S["s_t1st"], 16 * 8)
                g_.wait_ge(S["s_ssq2"], 16)
                with g_.register("p_g") as rp, g_.register("foff") as rf, \
                     g_.register("poff") as rpo, g_.register("cnt") as rc, \
                     g_.register("f0") as r0, g_.register("f1") as r1, \
                     g_.register("n0") as rn0, g_.register("n1") as rn1, \
                     g_.register("e0") as re0, g_.register("nd") as rnd:
                    g_.reg_load(rp, pf[:])
                    g_.reg_load(rn0, noncb[0:1, 0:1])
                    g_.reg_load(rn1, noncb[0:1, 1:2])
                    g_.reg_alu(rf, rp, 2, OP.mult)
                    g_.reg_alu(rpo, rp, -2, OP.mult)
                    g_.reg_alu(rpo, rpo, 2, OP.add)
                    fdst = bass.AP(shm_flag, rf, [[2, 1], [1, 2]])
                    g_.dma_start(out=fdst, in_=noncb[0:1, 0:2]).then_inc(S["g_flag"], 16)
                    g_.wait_ge(S["g_flag"], 16)
                    g_.reg_mov(rc, 16)
                    g_.reg_mov(rnd, 1)
                    fsrc = bass.AP(shm_flag, rpo, [[2, 1], [1, 2]])
                    with g_.While(rnd):
                        g_.reg_alu(rc, rc, 16, OP.add)
                        g_.dma_start(out=flagr[:], in_=fsrc).then_inc(S["g_flag"], 16)
                        g_.wait_ge(S["g_flag"], rc)
                        g_.reg_load(r0, flagr[:, 0:1])
                        g_.reg_load(r1, flagr[:, 1:2])
                        g_.reg_alu(re0, r0, rn0, OP.is_equal)
                        g_.reg_alu(rnd, r1, rn1, OP.is_equal)
                        g_.reg_alu(rnd, rnd, re0, OP.bitwise_and)
                        g_.reg_alu(rnd, rnd, 1, OP.bitwise_xor)
                    g_.sem_inc(S["g_xchg"], 1)

        esP2.close()
        esDT.close()

        if upto == 2:
            with nc.Block() as bex:
                @bex.sync
                def _(sy):
                    sy.wait_ge(S["g_xchg"], 1)
                    sy.dma_start(out=out_h.ap()[0:128, 0:128],
                                 in_=idf[:]).then_inc(S["s_out"], 16)
                    sy.wait_ge(S["s_out"], 16)
            return nc

        esP3 = ExitStack()
        EP3 = esP3.enter_context
        g_sb = EP3(nc.sbuf_tensor("g_sb", [128, 32 * 1024], BF16))
        wob = [EP3(nc.sbuf_tensor(f"wob{i}", [128, 32 * 128], BF16)) for i in range(2)]
        rstdb = EP3(nc.sbuf_tensor("rstdb", [128, 1024], F32))
        sqa = EP3(nc.sbuf_tensor("sqa", [1, TH], F32))
        sqb = EP3(nc.sbuf_tensor("sqb", [1, TH], F32))
        rows = EP3(nc.sbuf_tensor("rows", [1, 5 * 1024], F32))
        xtht = EP3(nc.sbuf_tensor("xtht", [128, 2 * 512], F32))
        ascr = EP3(nc.sbuf_tensor("ascr", [128, 2 * 512], F32))
        x1scr = EP3(nc.sbuf_tensor("x1scr", [128, 2 * 512], F32))
        rbt = EP3(nc.sbuf_tensor("rbt", [128, 2 * 512], BF16))
        rqt = EP3(nc.sbuf_tensor("rqt", [128, 2 * 512], BF16))
        p_u0 = EP3(nc.psum_tensor("p_u0", [128, 512], F32))
        p_u1 = EP3(nc.psum_tensor("p_u1", [128, 512], F32))
        p_sm = EP3(nc.psum_tensor("p_sm", [1, 1024], F32))
        p_sq = EP3(nc.psum_tensor("p_sq", [1, 1024], F32))

        with nc.Block() as b3:
            @b3.sync
            def _(sy):
                if True:
                    fwc = xthc = r1c = x1c = 0

                    def oinc(sem, cnt):
                        if cnt >= 1:
                            sy.wait_ge(S[sem], 16 * cnt)
                        return cnt + 1
                    sy.wait_ge(S["s_pc"], 16 * 16)
                    sy.wait_ge(S["g_xchg"], 1)
                    fwc = oinc("s_fw", fwc)
                    sy.dma_start(out=wob[0][:],
                                 in_=wo_bf.ap()[0]).then_inc(S["s_fw"], 16)
                    fwc = oinc("s_fw", fwc)
                    sy.dma_start(out=wob[1][:],
                                 in_=wo_bf.ap()[1]).then_inc(S["s_fw"], 16)
                    for tb3 in range(2):
                        for ot in range(16):
                            gi = tb3 * 16 + ot
                            for tq in range(2):
                                K = tb3 * 32 + ot * 2 + tq
                                if K >= 2:
                                    sy.wait_ge(S["v_att"], K - 1)
                                xthc = oinc("s_xth", xthc)
                                sy.dma_start(
                                    out=xtht[:, (K % 2) * 512:(K % 2 + 1) * 512],
                                    in_=xTh_h.ap()[ot * 128:(ot + 1) * 128,
                                                   tb3 * 1024 + tq * 512:
                                                   tb3 * 1024 + (tq + 1) * 512]
                                ).then_inc(S["s_xth"], 16)
                                sy.wait_ge(S["v_att"], K + 1)
                                r1c = oinc("s_r1", r1c)
                                sy.dma_start(
                                    out=sp_r1.ap()[ot, :,
                                                   tb3 * 1024 + tq * 512:
                                                   tb3 * 1024 + (tq + 1) * 512],
                                    in_=ascr[:, (K % 2) * 512:(K % 2 + 1) * 512]
                                ).then_inc(S["s_r1"], 16)
                            gnext = gi + 2
                            if gnext <= 31:
                                sy.wait_ge(S["s_gu"], 2 * (gi + 1))
                                fwc = oinc("s_fw", fwc)
                                sy.dma_start(out=wob[gnext % 2][:],
                                             in_=wo_bf.ap()[gnext - 16 if gnext >= 16 else gnext]
                                             ).then_inc(S["s_fw"], 16)
                        sy.wait_ge(S["s_r1"], 16 * 32 * (tb3 + 1))
                        for tq in range(2):
                            for ot in range(16):
                                K2 = tb3 * 32 + tq * 16 + ot
                                if K2 >= 2:
                                    sy.wait_ge(S["v_x1"], K2 - 1)
                                xthc = oinc("s_xth", xthc)
                                sy.dma_start(
                                    out=xtht[:, (K2 % 2) * 512:(K2 % 2 + 1) * 512],
                                    in_=sp_r1.ap()[ot, :,
                                                   tb3 * 1024 + tq * 512:
                                                   tb3 * 1024 + (tq + 1) * 512]
                                ).then_inc(S["s_xth"], 16)
                                sy.wait_ge(S["v_x1"], K2 + 1)
                                x1c = oinc("s_x1t", x1c)
                                sy.dma_start(
                                    out=sp_x1.ap()[ot, :,
                                                   tb3 * 1024 + tq * 512:
                                                   tb3 * 1024 + (tq + 1) * 512],
                                    in_=x1scr[:, (K2 % 2) * 512:(K2 % 2 + 1) * 512]
                                ).then_inc(S["s_x1t"], 16)

            @b3.gpsimd
            def _(g_):
                g_.wait_ge(S["g_xchg"], 1)
                with g_.register("p3p") as rp, g_.register("r1_") as r1, \
                     g_.register("r2_") as r2, g_.register("r3_") as r3:
                    g_.reg_load(rp, pf[:])
                    g_.reg_alu(r1, rp, TH, OP.mult)
                    sA = bass.AP(shm_ssq, r1, [[T, 1], [1, TH]])
                    g_.dma_start(out=sqa[:], in_=sA).then_inc(S["s_g"], 16)
                    g_.reg_alu(r3, r1, T, OP.add)
                    sB = bass.AP(shm_ssq, r3, [[T, 1], [1, TH]])
                    g_.wait_ge(S["s_g"], 16)
                    g_.dma_start(out=sqb[:], in_=sB).then_inc(S["s_g"], 16)
                    for tb3 in range(2):
                        if tb3 == 1:
                            g_.wait_ge(S["s_gu"], 32)
                        g_.reg_alu(r2, rp, SLOT_T1 + TH, OP.mult)
                        gdst = g_sb[:].rearrange("p (f t) -> p f t", t=1024)
                        g_.reg_alu(r3, r2, tb3 * 1024, OP.add)
                        srcA = bass.AP(shm_t1, r3,
                                       [[T, 128], [128 * T, NF], [1, 1024]])
                        g_.wait_ge(S["s_g"], 32 + tb3 * 32)
                        g_.dma_start(out=gdst[:, 0:16, :],
                                     in_=srcA).then_inc(S["s_g"], 16)
                        g_.reg_alu(r1, rp, TH - SLOT_T1, OP.mult)
                        g_.reg_alu(r1, r1, SLOT_T1, OP.add)
                        g_.reg_alu(r1, r1, tb3 * 1024, OP.add)
                        srcB = bass.AP(shm_t1, r1,
                                       [[T, 128], [128 * T, NF], [1, 1024]])
                        g_.wait_ge(S["s_g"], 48 + tb3 * 32)
                        g_.dma_start(out=gdst[:, 16:32, :],
                                     in_=srcB).then_inc(S["s_g"], 16)

            @b3.vector
            def _(v):
                v.wait_ge(S["s_g"], 32)
                v.tensor_add(sqa[:], sqa[:], sqb[:])
                v.drain()
                v.tensor_scalar(sqa[:], sqa[:], 1.0 / 4096.0, EPS_RMS,
                                OP.mult, OP.add)
                v.drain()
                v.sem_inc(S["v_rstd"], 1)
                v.wait_ge(S["a_rstd"], 1)
                v.reciprocal(sqa[:], sqa[:])
                v.drain()
                v.sem_inc(S["v_rstd"], 1)
                for tb3 in range(2):
                    for tq in range(2):
                        v.wait_ge(S["s_brc"], tb3 * 6 + tq + 1)
                        v.tensor_copy(rstdb[:, tq * 512:(tq + 1) * 512],
                                      p_u0[:] if tq == 0 else p_u1[:])
                    v.drain()
                    v.sem_inc(S["v_rstd"], 1)
                    for ot in range(16):
                        for tq in range(2):
                            K = tb3 * 32 + ot * 2 + tq
                            pp = p_a if K % 2 == 0 else p_b
                            v.wait_ge(S["s_gu"], K + 1)
                            v.wait_ge(S["s_xth"], 16 * (tb3 * 64 + ot * 2 + tq + 1))
                            if K >= 2:
                                v.wait_ge(S["s_r1"], 16 * (K - 1))
                            v.tensor_tensor(ascr[:, (K % 2) * 512:(K % 2 + 1) * 512],
                                            pp[:, 0:512],
                                            rstdb[:, tq * 512:(tq + 1) * 512],
                                            OP.mult)
                            v.drain()
                            v.tensor_add(ascr[:, (K % 2) * 512:(K % 2 + 1) * 512],
                                         ascr[:, (K % 2) * 512:(K % 2 + 1) * 512],
                                         xtht[:, (K % 2) * 512:(K % 2 + 1) * 512])
                            v.drain()
                            if K >= 2:
                                v.wait_ge(S["s_qmv"], 2 * (K - 1))
                            v.tensor_copy(rbt[:, (K % 2) * 512:(K % 2 + 1) * 512],
                                          ascr[:, (K % 2) * 512:(K % 2 + 1) * 512])
                            v.drain()
                            v.sem_inc(S["v_att"], 1)
                    v.wait_ge(S["s_qmv"], 2 * 32 * (tb3 + 1))
                    v.tensor_scalar_mul(rows[0:1, 0:1024], p_sm[0:1, :], 1.0 / DM)
                    v.tensor_scalar_mul(rows[0:1, 1024:2048], p_sq[0:1, :], 1.0 / DM)
                    v.drain()
                    v.tensor_mul(rows[0:1, 4096:5120], rows[0:1, 0:1024],
                                 rows[0:1, 0:1024])
                    v.drain()
                    v.tensor_sub(rows[0:1, 1024:2048], rows[0:1, 1024:2048],
                                 rows[0:1, 4096:5120])
                    v.drain()
                    v.tensor_scalar(rows[0:1, 1024:2048], rows[0:1, 1024:2048],
                                    1.0, EPS_LN, OP.mult, OP.add)
                    v.drain()
                    v.sem_inc(S["v_row"], 1)
                    v.wait_ge(S["a_row"], tb3 + 1)
                    v.reciprocal(rows[0:1, 2048:3072], rows[0:1, 1024:2048])
                    v.drain()
                    v.tensor_mul(rows[0:1, 3072:4096], rows[0:1, 0:1024],
                                 rows[0:1, 2048:3072])
                    v.drain()
                    v.tensor_scalar_mul(rows[0:1, 3072:4096],
                                        rows[0:1, 3072:4096], -1.0)
                    v.drain()
                    v.sem_inc(S["v_row"], 1)
                    for tq in range(2):
                        for ot in range(16):
                            K2 = tb3 * 32 + tq * 16 + ot
                            v.wait_ge(S["s_brc"], tb3 * 6 + 2 + 2 * (tq + 1))
                            v.wait_ge(S["s_xth"],
                                      16 * (tb3 * 64 + 32 + tq * 16 + ot + 1))
                            if K2 >= 2:
                                v.wait_ge(S["s_x1t"], 16 * (K2 - 1))
                            v.tensor_tensor(
                                x1scr[:, (K2 % 2) * 512:(K2 % 2 + 1) * 512],
                                xtht[:, (K2 % 2) * 512:(K2 % 2 + 1) * 512],
                                p_u0[:], OP.mult)
                            v.drain()
                            v.tensor_tensor(
                                x1scr[:, (K2 % 2) * 512:(K2 % 2 + 1) * 512],
                                x1scr[:, (K2 % 2) * 512:(K2 % 2 + 1) * 512],
                                p_u1[:], OP.add)
                            v.drain()
                            v.sem_inc(S["v_x1"], 1)

            @b3.scalar
            def _(a):
                a.wait_ge(S["v_rstd"], 1)
                a.activation(sqa[:], sqa[:], AF.Sqrt)
                a.drain()
                a.sem_inc(S["a_rstd"], 1)
                for tb3 in range(2):
                    for ot in range(16):
                        for tq in range(2):
                            K = tb3 * 32 + ot * 2 + tq
                            a.wait_ge(S["v_att"], K + 1)
                            if K >= 2:
                                a.wait_ge(S["s_qmv"], 2 * (K - 1))
                            a.activation(rqt[:, (K % 2) * 512:(K % 2 + 1) * 512],
                                         rbt[:, (K % 2) * 512:(K % 2 + 1) * 512],
                                         AF.Square)
                            a.drain()
                            a.sem_inc(S["a_sq2"], 1)
                    a.wait_ge(S["v_row"], 2 * tb3 + 1)
                    a.activation(rows[0:1, 1024:2048], rows[0:1, 1024:2048], AF.Sqrt)
                    a.drain()
                    a.sem_inc(S["a_row"], 1)

            @b3.tensor
            def _(t):
                gv = g_sb[:].rearrange("p (f t) -> p f t", t=1024)
                t.wait_ge(S["v_yev"], 512)
                t.wait_ge(S["a_exp"], 512)
                for tb3 in range(2):
                    t.wait_ge(S["v_rstd"], 2)
                    if tb3 == 1:
                        t.wait_ge(S["v_x1"], 32)
                    t.matmul(p_u0[:], orf[:],
                             sqa[0:1, tb3 * 1024:tb3 * 1024 + 512],
                             start=True, stop=True).then_inc(S["s_brc"], 1)
                    t.matmul(p_u1[:], orf[:],
                             sqa[0:1, tb3 * 1024 + 512:tb3 * 1024 + 1024],
                             start=True, stop=True).then_inc(S["s_brc"], 1)
                    t.wait_ge(S["s_g"], 32 + 32 * (tb3 + 1))
                    for ot in range(16):
                        gi = tb3 * 16 + ot
                        t.wait_ge(S["s_fw"], 16 * (gi + 1))
                        wv = wob[gi % 2][:].rearrange("p (ic o) -> p ic o", o=128)
                        for tq in range(2):
                            K = tb3 * 32 + ot * 2 + tq
                            pp = p_a if K % 2 == 0 else p_b
                            if K >= 2:
                                t.wait_ge(S["v_att"], K - 1)
                            for ic in range(32):
                                mm = t.matmul(pp[:, 0:512], wv[:, ic, :],
                                              gv[:, ic, tq * 512:(tq + 1) * 512],
                                              start=(ic == 0), stop=(ic == 31))
                            mm.then_inc(S["s_gu"], 1)
                            t.wait_ge(S["a_sq2"], K + 1)
                            t.matmul(p_sm[0:1, tq * 512:(tq + 1) * 512], ocb[:],
                                     rbt[:, (K % 2) * 512:(K % 2 + 1) * 512],
                                     start=(ot == 0), stop=(ot == 15))
                            t.matmul(p_sq[0:1, tq * 512:(tq + 1) * 512], ocb[:],
                                     rqt[:, (K % 2) * 512:(K % 2 + 1) * 512],
                                     start=(ot == 0),
                                     stop=(ot == 15)).then_inc(S["s_qmv"], 2)
                    t.wait_ge(S["v_row"], 2 * tb3 + 2)
                    for tq in range(2):
                        if tb3 * 2 + tq >= 1:
                            t.wait_ge(S["v_x1"], 16 * (tb3 * 2 + tq))
                        t.matmul(p_u0[:], orf[:],
                                 rows[0:1, 2048 + tq * 512:2048 + (tq + 1) * 512],
                                 start=True, stop=True)
                        t.matmul(p_u1[:], orf[:],
                                 rows[0:1, 3072 + tq * 512:3072 + (tq + 1) * 512],
                                 start=True, stop=True).then_inc(S["s_brc"], 2)
        esP3.close()

        if upto == 3:
            with nc.Block() as bex:
                @bex.sync
                def _(sy):
                    sy.wait_ge(S["s_x1t"], 16 * 64)
                    sy.dma_start(out=out_h.ap()[0:128, 0:128],
                                 in_=idf[:]).then_inc(S["s_out"], 16)
                    sy.wait_ge(S["s_out"], 16)
            return nc

        esP5 = ExitStack()
        EP5 = esP5.enter_context
        h_sb = EP5(nc.sbuf_tensor("h_sb", [128, 64 * 512], BF16))
        x1b = EP5(nc.sbuf_tensor("x1b", [128, 16 * 512], BF16))
        wp = [EP5(nc.sbuf_tensor(f"wp{i}", [128, 16 * 256], BF16)) for i in range(2)]
        wd = [EP5(nc.sbuf_tensor(f"wd{i}", [128, 64 * 128], BF16)) for i in range(2)]
        rows5 = EP5(nc.sbuf_tensor("rows5", [1, 5 * 512], F32))
        xt5 = EP5(nc.sbuf_tensor("xt5", [128, 2 * 512], F32))
        r2scr = EP5(nc.sbuf_tensor("r2scr", [128, 2 * 512], F32))
        o5scr = EP5(nc.sbuf_tensor("o5scr", [128, 2 * 512], F32))
        rb5 = EP5(nc.sbuf_tensor("rb5", [128, 2 * 512], BF16))
        rq5 = EP5(nc.sbuf_tensor("rq5", [128, 2 * 512], BF16))
        sg5 = EP5(nc.sbuf_tensor("sg5", [128, 2 * 512], BF16))
        p_u0b = EP5(nc.psum_tensor("p_u0b", [128, 512], F32))
        p_u1b = EP5(nc.psum_tensor("p_u1b", [128, 512], F32))
        p_sm5 = EP5(nc.psum_tensor("p_sm5", [1, 512], F32))
        p_sq5 = EP5(nc.psum_tensor("p_sq5", [1, 512], F32))

        NFW0 = 32 * 16
        XTH0 = 128
        for tb in range(4):
            with nc.Block() as b5a:
                @b5a.gpsimd
                def _(g_, tb=tb):
                    if tb == 0:
                        g_.wait_ge(S["s_x1t"], 16 * 64)
                    else:
                        g_.wait_ge(S["s_gu"], 64 + 64 * tb)
                        g_.wait_ge(S["g_x1b"], 16 * tb)
                    src = sp_x1.ap()[:, :, tb * 512:(tb + 1) * 512] \
                        .rearrange("f p t -> p f t")
                    dst = x1b[:].rearrange("p (f t) -> p f t", t=512)
                    g_.dma_start(out=dst, in_=src).then_inc(S["g_x1b"], 16)
                    for ht in range(64):
                        hi = tb * 64 + ht
                        if hi >= 2:
                            g_.wait_ge(S["s_gu"], 64 + hi - 1)
                        dstw = wp[hi % 2][:].rearrange("p (ic o) -> p ic o", o=256)
                        if hi >= 1:
                            g_.wait_ge(S["g_wp"], 16 * 2 * hi)
                        g_.dma_start(
                            out=dstw[:, :, 0:128],
                            in_=wgT_h.ap()[ht].rearrange("p (ic o) -> p ic o", o=128)
                        ).then_inc(S["g_wp"], 16)
                        g_.wait_ge(S["g_wp"], 16 * (2 * hi + 1))
                        g_.dma_start(
                            out=dstw[:, :, 128:256],
                            in_=wuT_h.ap()[ht].rearrange("p (ic o) -> p ic o", o=128)
                        ).then_inc(S["g_wp"], 16)

                @b5a.tensor
                def _(t, tb=tb):
                    xv5 = x1b[:].rearrange("p (f t) -> p f t", t=512)
                    t.wait_ge(S["g_x1b"], 16 * (tb + 1))
                    if tb >= 1:
                        t.wait_ge(S["v_out"], 16 * tb)
                        t.wait_ge(S["v_r2"], 16 * tb)
                    for ht in range(64):
                        hi = tb * 64 + ht
                        t.wait_ge(S["g_wp"], 16 * 2 * (hi + 1))
                        wv = wp[hi % 2][:].rearrange("p (ic o) -> p ic o", o=256)
                        pg = p_a if ht % 2 == 0 else p_b
                        pu = p_u0b if ht % 2 == 0 else p_u1b
                        if ht >= 2:
                            t.wait_ge(S["v_h"], 64 * tb + ht - 1)
                        for ic in range(16):
                            t.matmul(pg[:, 0:512], wv[:, ic, 0:128],
                                     xv5[:, ic, :], start=(ic == 0), stop=(ic == 15))
                        for ic in range(16):
                            mm = t.matmul(pu[:, 0:512], wv[:, ic, 128:256],
                                          xv5[:, ic, :], start=(ic == 0),
                                          stop=(ic == 15))
                        mm.then_inc(S["s_gu"], 1)

                @b5a.scalar
                def _(a, tb=tb):
                    for ht in range(64):
                        hi = tb * 64 + ht
                        a.wait_ge(S["s_gu"], 64 + hi + 1)
                        if ht >= 2:
                            a.wait_ge(S["v_h"], 64 * tb + ht - 1)
                        pg = p_a if ht % 2 == 0 else p_b
                        a.activation(sg5[:, (ht % 2) * 512:(ht % 2 + 1) * 512],
                                     pg[:, 0:512], AF.Silu)
                        a.drain()
                        a.sem_inc(S["a_sg"], 1)

                @b5a.vector
                def _(v, tb=tb):
                    hv = h_sb[:].rearrange("p (f t) -> p f t", t=512)
                    for ht in range(64):
                        hi = tb * 64 + ht
                        v.wait_ge(S["a_sg"], hi + 1)
                        if tb >= 1 and ht == 0:
                            v.wait_ge(S["s_dmm"], 16 * tb)
                        pu = p_u0b if ht % 2 == 0 else p_u1b
                        v.tensor_tensor(hv[:, ht, :],
                                        sg5[:, (ht % 2) * 512:(ht % 2 + 1) * 512],
                                        pu[:, 0:512], OP.mult)
                        v.drain()
                        v.sem_inc(S["v_h"], 1)

            with nc.Block() as b5b:
                @b5b.gpsimd
                def _(g_, tb=tb):
                    for ot in range(16):
                        di = tb * 16 + ot
                        if di >= 2:
                            g_.wait_ge(S["s_dmm"], di - 1)
                        if di >= 1:
                            g_.wait_ge(S["s_dw5"], 16 * di)
                        g_.dma_start(
                            out=wd[di % 2][:],
                            in_=wdT_h.ap()[ot]).then_inc(S["s_dw5"], 16)

                @b5b.sync
                def _(sy, tb=tb):
                    if tb == 0:
                        sy.wait_ge(S["s_x1t"], 16 * 64)
                    for ot in range(16):
                        K = tb * 16 + ot
                        if K >= 2:
                            sy.wait_ge(S["v_r2"], K - 1)
                        sy.wait_ge(S["s_xth"], 16 * (XTH0 + 32 * tb + ot))
                        sy.dma_start(
                            out=xt5[:, (K % 2) * 512:(K % 2 + 1) * 512],
                            in_=sp_x1.ap()[ot, :, tb * 512:(tb + 1) * 512]
                        ).then_inc(S["s_xth"], 16)
                        sy.wait_ge(S["v_r2"], K + 1)
                        if K >= 1:
                            sy.wait_ge(S["s_r2t"], 16 * K)
                        sy.dma_start(
                            out=sp_r2.ap()[ot, :, tb * 512:(tb + 1) * 512],
                            in_=r2scr[:, (K % 2) * 512:(K % 2 + 1) * 512]
                        ).then_inc(S["s_r2t"], 16)
                    sy.wait_ge(S["s_r2t"], 16 * 16 * (tb + 1))
                    for ot in range(16):
                        K2 = tb * 16 + ot
                        if K2 >= 2:
                            sy.wait_ge(S["v_out"], K2 - 1)
                        sy.wait_ge(S["s_xth"], 16 * (XTH0 + 32 * tb + 16 + ot))
                        sy.dma_start(
                            out=xt5[:, (K2 % 2) * 512:(K2 % 2 + 1) * 512],
                            in_=sp_r2.ap()[ot, :, tb * 512:(tb + 1) * 512]
                        ).then_inc(S["s_xth"], 16)
                        sy.wait_ge(S["v_out"], K2 + 1)
                        if K2 >= 1:
                            sy.wait_ge(S["s_out"], 16 * K2)
                        sy.dma_start(
                            out=out_h.ap()[ot * 128:(ot + 1) * 128,
                                           tb * 512:(tb + 1) * 512],
                            in_=o5scr[:, (K2 % 2) * 512:(K2 % 2 + 1) * 512]
                        ).then_inc(S["s_out"], 16)
                    if tb == 3:
                        sy.wait_ge(S["s_out"], 16 * 64)

                @b5b.tensor
                def _(t, tb=tb):
                    hv = h_sb[:].rearrange("p (f t) -> p f t", t=512)
                    t.wait_ge(S["v_h"], 64 * (tb + 1))
                    for ot in range(16):
                        di = tb * 16 + ot
                        K = di
                        t.wait_ge(S["s_dw5"], 16 * (di + 1))
                        wv = wd[di % 2][:].rearrange("p (ic o) -> p ic o", o=128)
                        pp = p_a if K % 2 == 0 else p_b
                        if K >= 2:
                            t.wait_ge(S["v_r2"], K - 1)
                        for ic in range(64):
                            mm = t.matmul(pp[:, 0:512], wv[:, ic, :], hv[:, ic, :],
                                          start=(ic == 0), stop=(ic == 63))
                        mm.then_inc(S["s_dmm"], 1)
                        t.wait_ge(S["a_sq3"], K + 1)
                        t.matmul(p_sm5[0:1, :], ocb[:],
                                 rb5[:, (K % 2) * 512:(K % 2 + 1) * 512],
                                 start=(ot == 0), stop=(ot == 15))
                        t.matmul(p_sq5[0:1, :], ocb[:],
                                 rq5[:, (K % 2) * 512:(K % 2 + 1) * 512],
                                 start=(ot == 0),
                                 stop=(ot == 15)).then_inc(S["s_qm3"], 1)
                    t.wait_ge(S["v_row2"], 2 * tb + 2)
                    if tb >= 1:
                        t.wait_ge(S["v_out"], 16 * tb)
                    t.matmul(p_u0b[:], orf[:], rows5[0:1, 1024:1536],
                             start=True, stop=True)
                    t.matmul(p_u1b[:], orf[:], rows5[0:1, 1536:2048],
                             start=True, stop=True).then_inc(S["s_br2"], 2)

                @b5b.scalar
                def _(a, tb=tb):
                    for ot in range(16):
                        K = tb * 16 + ot
                        a.wait_ge(S["v_r2"], K + 1)
                        if K >= 2:
                            a.wait_ge(S["s_qm3"], K - 1)
                        a.activation(rq5[:, (K % 2) * 512:(K % 2 + 1) * 512],
                                     rb5[:, (K % 2) * 512:(K % 2 + 1) * 512],
                                     AF.Square)
                        a.drain()
                        a.sem_inc(S["a_sq3"], 1)
                    a.wait_ge(S["v_row2"], 2 * tb + 1)
                    a.activation(rows5[0:1, 512:1024], rows5[0:1, 512:1024], AF.Sqrt)
                    a.drain()
                    a.sem_inc(S["a_row2"], 1)

                @b5b.vector
                def _(v, tb=tb):
                    for ot in range(16):
                        K = tb * 16 + ot
                        v.wait_ge(S["s_dmm"], K + 1)
                        v.wait_ge(S["s_xth"], 16 * (XTH0 + tb * 32 + ot + 1))
                        pp = p_a if K % 2 == 0 else p_b
                        if K >= 2:
                            v.wait_ge(S["s_r2t"], 16 * (K - 1))
                        v.tensor_add(r2scr[:, (K % 2) * 512:(K % 2 + 1) * 512],
                                     pp[:, 0:512],
                                     xt5[:, (K % 2) * 512:(K % 2 + 1) * 512])
                        v.drain()
                        if K >= 2:
                            v.wait_ge(S["s_qm3"], K - 1)
                        v.tensor_copy(rb5[:, (K % 2) * 512:(K % 2 + 1) * 512],
                                      r2scr[:, (K % 2) * 512:(K % 2 + 1) * 512])
                        v.drain()
                        v.sem_inc(S["v_r2"], 1)
                    v.wait_ge(S["s_qm3"], 16 * (tb + 1))
                    v.tensor_scalar_mul(rows5[0:1, 0:512], p_sm5[0:1, :], 1.0 / DM)
                    v.tensor_scalar_mul(rows5[0:1, 512:1024], p_sq5[0:1, :], 1.0 / DM)
                    v.drain()
                    v.tensor_mul(rows5[0:1, 2048:2560], rows5[0:1, 0:512],
                                 rows5[0:1, 0:512])
                    v.drain()
                    v.tensor_sub(rows5[0:1, 512:1024], rows5[0:1, 512:1024],
                                 rows5[0:1, 2048:2560])
                    v.drain()
                    v.tensor_scalar(rows5[0:1, 512:1024], rows5[0:1, 512:1024],
                                    1.0, EPS_LN, OP.mult, OP.add)
                    v.drain()
                    v.sem_inc(S["v_row2"], 1)
                    v.wait_ge(S["a_row2"], tb + 1)
                    v.reciprocal(rows5[0:1, 1024:1536], rows5[0:1, 512:1024])
                    v.drain()
                    v.tensor_mul(rows5[0:1, 1536:2048], rows5[0:1, 0:512],
                                 rows5[0:1, 1024:1536])
                    v.drain()
                    v.tensor_scalar_mul(rows5[0:1, 1536:2048],
                                        rows5[0:1, 1536:2048], -1.0)
                    v.drain()
                    v.sem_inc(S["v_row2"], 1)
                    for ot in range(16):
                        K2 = tb * 16 + ot
                        v.wait_ge(S["s_br2"], 2 * (tb + 1))
                        v.wait_ge(S["s_xth"], 16 * (XTH0 + tb * 32 + 16 + ot + 1))
                        if K2 >= 2:
                            v.wait_ge(S["s_out"], 16 * (K2 - 1))
                        v.tensor_tensor(
                            o5scr[:, (K2 % 2) * 512:(K2 % 2 + 1) * 512],
                            xt5[:, (K2 % 2) * 512:(K2 % 2 + 1) * 512],
                            p_u0b[:], OP.mult)
                        v.drain()
                        v.tensor_tensor(
                            o5scr[:, (K2 % 2) * 512:(K2 % 2 + 1) * 512],
                            o5scr[:, (K2 % 2) * 512:(K2 % 2 + 1) * 512],
                            p_u1b[:], OP.add)
                        v.drain()
                        v.sem_inc(S["v_out"], 1)
        esP5.close()
    return nc



def _prep_inputs(inputs):
    x = np.asarray(inputs["L0_attn_x"], np.float32)
    W_in = np.asarray(inputs["L0_attn_W_in"], np.float32)
    conv_w = np.asarray(inputs["L0_attn_conv_w"], np.float32)
    conv_b = np.asarray(inputs["L0_attn_conv_b"], np.float32)
    A_log = np.asarray(inputs["L0_attn_A_log"], np.float32)
    dt_bias = np.asarray(inputs["L0_attn_dt_bias"], np.float32)
    Dv = np.asarray(inputs["L0_attn_D"], np.float32)
    W_out = np.asarray(inputs["L0_attn_W_out"], np.float32)
    w_gate = np.asarray(inputs["L0_ffn_w_gate"], np.float32)
    w_up = np.asarray(inputs["L0_ffn_w_up"], np.float32)
    w_down = np.asarray(inputs["L0_ffn_w_down"], np.float32)

    def swz(WT, n_out_tiles, K):
        O = WT.shape[1]
        a = WT.reshape(K // 128, 128, n_out_tiles, O // n_out_tiles)
        return np.ascontiguousarray(a.transpose(2, 1, 0, 3).reshape(
            n_out_tiles, 128, (K // 128) * (O // n_out_tiles)))

    wgT = swz(np.ascontiguousarray(w_gate.T), 64, DM)
    wuT = swz(np.ascontiguousarray(w_up.T), 64, DM)
    wdT = swz(np.ascontiguousarray(w_down.T), 16, DH)

    in_maps = []
    rng = np.random.default_rng()
    for s in range(4):
        nonce_pair = rng.integers(1, 2**31 - 9, (4, 2), dtype=np.int32)
        xs = np.ascontiguousarray(x[s].T)
        for p in range(2):
            z_rows = np.arange(p * 2048, (p + 1) * 2048)
            xs_rows = np.arange(4096 + p * 2048, 4096 + (p + 1) * 2048)
            bc_rows = np.arange(8192, 8320)
            dt_rows = np.arange(8320 + p * 16, 8320 + (p + 1) * 16)
            rows_sel = np.concatenate([z_rows, xs_rows, bc_rows, dt_rows])
            winT0 = np.zeros((DM, WIN_COLS), np.float32)
            winT0[:, :rows_sel.size] = W_in[rows_sel, :].T
            winT = winT0.reshape(16, 128, NWG, WG).transpose(2, 1, 0, 3) \
                .reshape(NWG, 128, 16 * WG)
            my = np.arange(p * 2048, (p + 1) * 2048)
            pa = np.arange((1 - p) * 2048, (2 - p) * 2048)
            woutT = swz(np.ascontiguousarray(W_out[:, np.concatenate([my, pa])].T),
                        16, 2 * DM)
            cf = np.concatenate([xs_rows - 4096, np.arange(4096, 4224)])
            cw = conv_w[cf]
            cb = conv_b[cf]
            convp = np.zeros((128, NCT, 5), np.float32)
            for ftile in range(17):
                convp[:, ftile, 0:4] = cw[ftile * 128:(ftile + 1) * 128]
                convp[:, ftile, 4] = cb[ftile * 128:(ftile + 1) * 128]
            convp[0:64, 17, 0:4] = cw[2112:2176]
            convp[0:64, 17, 4] = cb[2112:2176]
            hcol = np.stack([A_log[p * 16:(p + 1) * 16],
                             dt_bias[p * 16:(p + 1) * 16]], axis=1)
            hrow = Dv[p * 16:(p + 1) * 16][None, :]
            xth = np.ascontiguousarray(x[s, p * 2048:(p + 1) * 2048, :].T)
            in_maps.append({
                "xT": xs, "xTh": np.ascontiguousarray(xth),
                "winT": np.ascontiguousarray(winT),
                "woutT": woutT, "wgT": wgT, "wuT": wuT, "wdT": wdT,
                "convp": np.ascontiguousarray(convp.reshape(128, NCT * 5)),
                "hcol": np.ascontiguousarray(hcol),
                "hrow": np.ascontiguousarray(hrow),
                "par": np.array([[p]], np.int32),
                "nonce": nonce_pair,
            })
    return in_maps


def kernel(**inputs):
    in_maps = _prep_inputs(inputs)
    nc = build_nc()
    res = run_bass_kernel_spmd(nc, in_maps, core_ids=list(range(8)))
    out = np.zeros((4, T, DM), np.float32)
    for s in range(4):
        for p in range(2):
            o = res.results[2 * s + p]["outT"]
            out[s, p * 2048:(p + 1) * 2048, :] = o.T
    return out


# version 10
# speedup vs baseline: 1.4131x; 1.4131x over previous
import numpy as np
import ml_dtypes
import concourse.bass as bass
import concourse.mybir as mybir
from concourse.bass_utils import run_bass_kernel_spmd
from contextlib import ExitStack

BF = ml_dtypes.bfloat16
F32 = mybir.dt.float32
BF16 = mybir.dt.bfloat16
I32 = mybir.dt.int32
AF = mybir.ActivationFunctionType
OP = mybir.AluOpType

T = 4096
TH = 2048
DM = 2048
NF = 16
NC_CH = 32
SC = 512
NCT = 18
WIN_COLS = 4352
WG = 256
NWG = WIN_COLS // WG
EPS_RMS = 1.1920929e-07
EPS_LN = 1e-5
DH = 8192
SLOT_T1 = NF * 128 * T
DETECT_RACES = False


def _consts():
    j = np.arange(128)
    triext = np.zeros((128, 256), np.float32)
    triext[:, 0:128] = (j[:, None] > j[None, :])
    triext[:, 128:192] = 1.0
    uincl = (j[:, None] <= j[None, :]).astype(np.float32)
    return {
        "id_bf": np.eye(128, dtype=BF),
        "id_f32": np.eye(128, dtype=np.float32),
        "triext": triext.astype(BF),
        "uincl": uincl.astype(BF),
        "ones_col_bf": np.ones((128, 1), dtype=BF),
        "ones_row_f": np.ones((1, 128), np.float32),
    }


def build_nc(upto=9):
    nc = bass.Bass("TRN2", target_bir_lowering=False, debug=False,
                   detect_race_conditions=DETECT_RACES)

    xT_h = nc.dram_tensor("xT", [DM, T], F32, kind="ExternalInput")
    xTh_h = nc.dram_tensor("xTh", [DM, TH], F32, kind="ExternalInput")
    winT_h = nc.dram_tensor("winT", [NWG, 128, 16 * WG], F32, kind="ExternalInput")
    woutT_h = nc.dram_tensor("woutT", [16, 128, 32 * 128], F32, kind="ExternalInput")
    wgT_h = nc.dram_tensor("wgT", [64, 128, 16 * 128], F32, kind="ExternalInput")
    wuT_h = nc.dram_tensor("wuT", [64, 128, 16 * 128], F32, kind="ExternalInput")
    wdT_h = nc.dram_tensor("wdT", [16, 128, 64 * 128], F32, kind="ExternalInput")
    convp_h = nc.dram_tensor("convp", [128, NCT * 5], F32, kind="ExternalInput")
    hcol_h = nc.dram_tensor("hcol", [16, 2], F32, kind="ExternalInput")
    hrow_h = nc.dram_tensor("hrow", [1, 16], F32, kind="ExternalInput")
    par_h = nc.dram_tensor("par", [1, 1], I32, kind="ExternalInput")
    non_h = nc.dram_tensor("nonce", [4, 2], I32, kind="ExternalInput")
    out_h = nc.dram_tensor("outT", [DM, TH], F32, kind="ExternalOutput")

    sp_z = nc.dram_tensor("sp_z", [NF, 128, T], BF16)
    sp_xbc = nc.dram_tensor("sp_xbc", [17, 128, T + 3], BF16)
    sp_r1 = nc.dram_tensor("sp_r1", [NF, 128, TH], F32)
    sp_x1 = nc.dram_tensor("sp_x1", [NF, 128, TH], F32)
    sp_r2 = nc.dram_tensor("sp_r2", [NF, 128, TH], F32)
    sp_ssq = nc.dram_tensor("sp_ssq", [1, T], F32)

    wo_bf = nc.dram_tensor("wo_bf", [16, 128, 32 * 128], BF16)
    shm_t1 = nc.dram_tensor("shm_t1", [2, NF, 128, T], BF16, addr_space="Shared")
    shm_ssq = nc.dram_tensor("shm_ssq", [2, 1, T], F32, addr_space="Shared")
    shm_flag = nc.dram_tensor("shm_flag", [2, 2], I32, addr_space="Shared")

    cst = _consts()
    c_idbf = nc.inline_tensor(cst["id_bf"], name="c_idbf")
    c_idf = nc.inline_tensor(cst["id_f32"], name="c_idf")
    c_tri = nc.inline_tensor(cst["triext"], name="c_tri")
    c_u = nc.inline_tensor(cst["uincl"], name="c_u")
    c_ocb = nc.inline_tensor(cst["ones_col_bf"], name="c_ocb")
    c_orf = nc.inline_tensor(cst["ones_row_f"], name="c_orf")

    with ExitStack() as es:
        E = es.enter_context
        idb = E(nc.sbuf_tensor("idb", [128, 128], BF16))
        idf = E(nc.sbuf_tensor("idf", [128, 128], F32))
        tri = E(nc.sbuf_tensor("tri", [128, 256], BF16))
        uin = E(nc.sbuf_tensor("uin", [128, 128], BF16))
        ocb = E(nc.sbuf_tensor("ocb", [128, 1], BF16))
        orf = E(nc.sbuf_tensor("orf", [1, 128], F32))
        convw = E(nc.sbuf_tensor("convw", [128, NCT * 5], F32))
        hc = E(nc.sbuf_tensor("hc", [16, 2], F32))
        Acol = E(nc.sbuf_tensor("Acol", [16, 1], F32))
        Db = E(nc.sbuf_tensor("Db", [128, 16], F32))
        pf = E(nc.sbuf_tensor("pf", [1, 1], I32))
        noncb = E(nc.sbuf_tensor("noncb", [4, 2], I32))
        flagr = E(nc.sbuf_tensor("flagr", [1, 2], I32))
        zeros16 = E(nc.sbuf_tensor("zeros16", [16, 128], F32))
        p_a = E(nc.psum_tensor("p_a", [128, 512], F32))
        p_b = E(nc.psum_tensor("p_b", [128, 512], F32))

        sem_names = [
            "g_cast", "s_load", "s_w", "s_wfree", "s_mm", "s_ev", "s_st", "s_sf",
            "s_dtr", "v_p1b", "a_p1b", "s_xbc", "v_conv", "a_xc", "s_tp", "v_tpe",
            "v_dmx", "s_E", "a_exp", "v_st2", "s_peh", "v_yev", "a_ysq", "s_qmm",
            "v_t1", "s_t1st", "g_flag", "g_xchg", "s_g", "v_rstd", "a_rstd",
            "v_att", "s_r1", "v_row", "a_row", "s_brc", "v_x1", "s_x1t",
            "g_x1b", "s_fw", "v_h", "a_sg", "s_gu", "v_r2", "s_r2t",
            "a_sq2", "s_qmv", "v_out", "s_out", "p0_pe", "p0_dve",
            "s_zld", "a_sz", "v_cdecb", "s_cdmm", "s_ssqst", "v_carry", "s_xth",
            "s_dw5", "s_dmm", "s_ssq2", "s_pc", "v_row2", "a_row2", "s_br2", "a_sq3", "s_qm3",
            "g_wp", "g_wu", "g_wp2", "g_wu2",
        ]
        S = {n: E(nc.semaphore(n)) for n in sem_names}

        with nc.Block() as b0:
            @b0.sync
            def _(sy):
                for t_, src in [(idb, c_idbf), (idf, c_idf), (tri, c_tri),
                                (uin, c_u), (ocb, c_ocb), (orf, c_orf),
                                (convw, convp_h), (hc, hcol_h), (pf, par_h),
                                (noncb, non_h)]:
                    sy.dma_start(out=t_[:], in_=src.ap()).then_inc(S["s_load"], 16)
                sy.dma_start(out=Db[0:1, :], in_=hrow_h.ap()).then_inc(S["s_load"], 16)

            @b0.scalar
            def _(a):
                a.wait_ge(S["s_load"], 176)
                a.activation(Acol[:], hc[:, 0:1], AF.Exp)
                a.drain()
                a.sem_inc(S["a_p1b"], 1)

            @b0.tensor
            def _(t):
                t.wait_ge(S["s_load"], 176)
                t.matmul(p_a[:, 0:16], orf[:], Db[0:1, :], start=True,
                         stop=True).then_inc(S["p0_pe"], 1)

            @b0.vector
            def _(v):
                v.wait_ge(S["s_load"], 176)
                v.memset(zeros16[:], 0.0)
                v.wait_ge(S["a_p1b"], 1)
                v.tensor_scalar_mul(Acol[:], Acol[:], -1.0)
                v.wait_ge(S["p0_pe"], 1)
                v.tensor_copy(Db[:], p_a[:, 0:16])
                v.drain()
                v.sem_inc(S["p0_dve"], 1)

        if upto == 0:
            with nc.Block() as bex:
                @bex.sync
                def _(sy):
                    sy.wait_ge(S["p0_dve"], 1)
                    sy.dma_start(out=out_h.ap()[0:128, 0:128],
                                 in_=idf[:]).then_inc(S["s_out"], 16)
                    sy.wait_ge(S["s_out"], 16)
            return nc

        esSC = ExitStack()
        scr16 = esSC.enter_context(nc.sbuf_tensor("scr16", [16, T], F32, side="right"))
        esP1 = ExitStack()
        EP1 = esP1.enter_context
        xbf = EP1(nc.sbuf_tensor("xbf", [128, NF * T], BF16))
        wbuf = [EP1(nc.sbuf_tensor(f"wbuf{i}", [128, 16 * WG], BF16)) for i in range(2)]
        stg = EP1(nc.sbuf_tensor("stg", [128, 4 * 512], BF16))
        stgf = EP1(nc.sbuf_tensor("stgf", [128, 2 * 512], F32))

        tiles = []
        spill_i = 0
        for g in range(NWG):
            for otl in range(2):
                ot = 2 * g + otl
                for tt in range(8):
                    sp = spill_i if ot <= 32 else None
                    if ot <= 32:
                        spill_i += 1
                    tiles.append((g, otl, ot, tt, sp))
        NT1 = len(tiles)
        NSP1 = spill_i

        with nc.Block() as b1:
            @b1.gpsimd
            def _(g_):
                g_.wait_ge(S["s_load"], 176)
                for i in range(4):
                    if i >= 1:
                        g_.wait_ge(S["g_cast"], 16 * i)
                    src = xT_h.ap()[i * 512:(i + 1) * 512, :].rearrange(
                        "(ic p) t -> p ic t", p=128)
                    dst = xbf[:].rearrange("p (ic t) -> p ic t", t=T)[:, i * 4:(i + 1) * 4, :]
                    g_.dma_start(out=dst, in_=src).then_inc(S["g_cast"], 16)
                for g in range(NWG):
                    if g >= 1:
                        g_.wait_ge(S["s_w"], 16 * g)
                    if g >= 2:
                        g_.wait_ge(S["s_mm"], 16 * (g - 1))
                    g_.dma_start(out=wbuf[g % 2][:],
                                 in_=winT_h.ap()[g]).then_inc(S["s_w"], 16)

            @b1.tensor
            def _(t):
                t.wait_ge(S["g_cast"], 64)
                t.wait_ge(S["p0_dve"], 1)
                skip_ev = (upto == 13)
                xv = xbf[:].rearrange("p (ic t) -> p ic t", t=T)
                ti = 0
                for g in range(NWG):
                    t.wait_ge(S["s_w"], 16 * (g + 1))
                    wv = wbuf[g % 2][:].rearrange("p (ic o) -> p ic o", o=WG)
                    for otl in range(2):
                        for tt in range(8):
                            pp = p_a if ti % 2 == 0 else p_b
                            if ti >= 2 and not skip_ev:
                                t.wait_ge(S["s_ev"], ti - 1)
                            for ic in range(16):
                                mm = t.matmul(pp[:, 0:512],
                                              wv[:, ic, otl * 128:(otl + 1) * 128],
                                              xv[:, ic, tt * 512:(tt + 1) * 512],
                                              start=(ic == 0), stop=(ic == 15))
                            mm.then_inc(S["s_mm"], 1)
                            ti += 1

            @b1.vector
            def _(v):
                if upto == 13:
                    return
                v.wait_ge(S["p0_dve"], 1)
                v.sem_inc(S["s_st"], 1)
                for ti, (g, otl, ot, tt, sp) in enumerate(tiles):
                    pp = p_a if ti % 2 == 0 else p_b
                    v.wait_ge(S["s_mm"], ti + 1)
                    if sp is not None:
                        slot = sp % 4
                        if sp >= 4 and upto not in (12, 14, 15, 16, 17):
                            v.wait_ge(S["s_sf"], 16 * (sp - 3))
                        if upto == 16:
                            v.tensor_copy(stgf[:, (sp % 2) * 512:(sp % 2 + 1) * 512],
                                          pp[:, 0:512])
                        elif upto == 17:
                            pass
                        else:
                            v.tensor_copy(stg[:, slot * 512:(slot + 1) * 512],
                                          pp[:, 0:512])
                        if upto != 15:
                            v.drain()
                        v.sem_inc(S["s_st"], 1)
                    elif ot == 33 and upto not in (14, 16, 17):
                        v.tensor_copy(scr16[0:16, tt * 512:(tt + 1) * 512],
                                      pp[0:16, 0:512])
                        v.drain()
                        v.sem_inc(S["s_dtr"], 1)
                    v.sem_inc(S["s_ev"], 1)

            @b1.sync
            def _(sy):
                if upto in (12, 13, 14, 15, 16, 17):
                    return
                sy.wait_ge(S["s_st"], 1)
                for ti, (g, otl, ot, tt, sp) in enumerate(tiles):
                    if sp is None:
                        continue
                    slot = sp % 4
                    sy.wait_ge(S["s_st"], sp + 2)
                    if sp >= 1:
                        sy.wait_ge(S["s_sf"], 16 * sp)
                    src = stg[:, slot * 512:(slot + 1) * 512]
                    if ot <= 15:
                        dst = sp_z.ap()[ot, :, tt * 512:(tt + 1) * 512]
                    else:
                        dst = sp_xbc.ap()[ot - 16, :, 3 + tt * 512:3 + (tt + 1) * 512]
                    sy.dma_start(out=dst, in_=src).then_inc(S["s_sf"], 16)

        esP1.close()

        esDT = ExitStack()
        EDT = esDT.enter_context
        dtq = EDT(nc.sbuf_tensor("dtq", [16, T], F32))
        dAq = EDT(nc.sbuf_tensor("dAq", [16, T], F32))
        wtl = EDT(nc.sbuf_tensor("wtl", [16, T], F32))
        alast = EDT(nc.sbuf_tensor("alast", [16, NC_CH], F32))

        if upto in (12, 13, 14, 15, 16, 17):
            with nc.Block() as bex:
                @bex.sync
                def _(sy):
                    sy.wait_ge(S["s_mm"], NT1)
                    if upto in (12, 14, 15, 16, 17):
                        sy.wait_ge(S["s_ev"], NT1)
                    sy.dma_start(out=out_h.ap()[0:128, 0:128],
                                 in_=idf[:]).then_inc(S["s_out"], 16)
                    sy.wait_ge(S["s_out"], 16)
            esDT.close()
            esSC.close()
            return nc

        if upto == 11:
            with nc.Block() as bex:
                @bex.sync
                def _(sy):
                    sy.wait_ge(S["s_sf"], 16 * NSP1)
                    sy.wait_ge(S["s_dtr"], 8)
                    sy.dma_start(out=out_h.ap()[0:128, 0:128],
                                 in_=idf[:]).then_inc(S["s_out"], 16)
                    sy.wait_ge(S["s_out"], 16)
            esDT.close()
            esSC.close()
            return nc

        with nc.Block() as b1b:
            @b1b.vector
            def _(v):
                v.wait_ge(S["s_dtr"], 8)
                v.tensor_scalar(dtq[:], scr16[0:16, :], hc[:, 1:2], None, OP.add)
                v.drain()
                v.sem_inc(S["v_p1b"], 1)
                v.wait_ge(S["a_p1b"], 4)
                v.tensor_scalar_max(scr16[0:16, :], dtq[:], 0.0)
                v.drain()
                v.tensor_add(dtq[:], scr16[0:16, :], dAq[:])
                v.drain()
                v.tensor_scalar(dAq[:], dtq[:], Acol[:], None, OP.mult)
                v.drain()
                for c in range(NC_CH):
                    v.tensor_tensor_scan(wtl[:, c * 128:(c + 1) * 128],
                                         dAq[:, c * 128:(c + 1) * 128],
                                         zeros16[:], 0.0, OP.add, OP.add)
                v.drain()
                al_src = bass.AP(wtl[:].tensor, wtl[:].offset + 127,
                                 [wtl[:].ap[0], [128, NC_CH]])
                v.tensor_copy(alast[:], al_src)
                v.drain()
                a3 = wtl[:].rearrange("h (c q) -> h c q", q=128)
                alb = bass.AP(alast[:].tensor, alast[:].offset,
                              [alast[:].ap[0], [1, NC_CH], [0, 128]])
                v.tensor_tensor(a3, a3, alb, OP.subtract)
                v.drain()
                v.sem_inc(S["v_p1b"], 1)
                v.wait_ge(S["a_p1b"], 6)
                v.tensor_mul(wtl[:], wtl[:], dtq[:])
                v.drain()
                v.sem_inc(S["v_p1b"], 1)

            @b1b.scalar
            def _(a):
                a.wait_ge(S["v_p1b"], 1)
                a.activation(dAq[:], dtq[:], AF.Abs)
                a.drain()
                a.sem_inc(S["a_p1b"], 1)
                a.activation(dAq[:], dAq[:], AF.Exp, scale=-1.0)
                a.drain()
                a.sem_inc(S["a_p1b"], 1)
                a.activation(dAq[:], dAq[:], AF.Ln, bias=1.0)
                a.drain()
                a.sem_inc(S["a_p1b"], 1)
                a.wait_ge(S["v_p1b"], 2)
                a.activation(wtl[:], wtl[:], AF.Exp, scale=-1.0)
                a.drain()
                a.sem_inc(S["a_p1b"], 1)
                a.activation(alast[:], alast[:], AF.Exp)
                a.drain()
                a.sem_inc(S["a_p1b"], 1)

        esSC.close()

        if upto == 1:
            esDT.close()
            with nc.Block() as bex:
                @bex.sync
                def _(sy):
                    sy.wait_ge(S["s_sf"], 16 * NSP1)
                    sy.wait_ge(S["v_p1b"], 3)
                    sy.dma_start(out=out_h.ap()[0:128, 0:51],
                                 in_=idf[:, 0:51]).then_inc(S["s_out"], 16)
                    sy.wait_ge(S["s_out"], 16)
            return nc

        esP2 = ExitStack()
        EP2 = esP2.enter_context
        carry = EP2(nc.sbuf_tensor("carry", [64, 16 * 128], F32))
        
        xbcs = EP2(nc.sbuf_tensor("xbcs", [128, NCT * (SC + 3)], BF16))
        accs = EP2(nc.sbuf_tensor("accs", [128, NCT * SC], BF16))
        cvt = EP2(nc.sbuf_tensor("cvt", [128, NCT * SC], BF16))
        xcs = [EP2(nc.sbuf_tensor(f"xcs{i}", [128, NCT * SC], BF16)) for i in range(2)]
        zc = [EP2(nc.sbuf_tensor(f"zc{i}", [128, NF * 128], BF16)) for i in range(2)]
        xt_sb = EP2(nc.sbuf_tensor("xt_sb", [128, 2 * 16 * 128], BF16))
        Bt_sb = EP2(nc.sbuf_tensor("Bt_sb", [128, 64], BF16))
        Gm_sb = EP2(nc.sbuf_tensor("Gm_sb", [128, 2 * 128], BF16))
        tch = EP2(nc.sbuf_tensor("tch", [128, 2 * 64], F32))
        cdecb = EP2(nc.sbuf_tensor("cdecb", [128, 2 * 16], F32))
        dmx = EP2(nc.sbuf_tensor("dmx", [128, 4 * 256], BF16))
        expe = EP2(nc.sbuf_tensor("expe", [128, 4 * 256], BF16))
        Stb = EP2(nc.sbuf_tensor("Stb", [128, 4 * 128], BF16))
        Svb = EP2(nc.sbuf_tensor("Svb", [64, 4 * 128], BF16))
        Bddb = EP2(nc.sbuf_tensor("Bddb", [128, 4 * 64], BF16))
        prevb = EP2(nc.sbuf_tensor("prevb", [64, 4 * 128], BF16))
        ybuf = EP2(nc.sbuf_tensor("ybuf", [128, 16 * 128], BF16))
        ysq = EP2(nc.sbuf_tensor("ysq", [128, 16 * 128], BF16))
        t1st = EP2(nc.sbuf_tensor("t1st", [128, 4 * NF * 128], BF16))
        ssqrow = EP2(nc.sbuf_tensor("ssqrow", [1, 128], F32))
        p_tA = EP2(nc.psum_tensor("p_tA", [128, 1024], BF16))
        p_tB = EP2(nc.psum_tensor("p_tB", [128, 1024], BF16))
        p_tC = EP2(nc.psum_tensor("p_tC", [128, 128], BF16))
        p_tD = EP2(nc.psum_tensor("p_tD", [128, 512], F32))
        p_e = [EP2(nc.psum_tensor(f"p_e{i}", [128, 512], F32)) for i in range(2)]
        p_e = p_e + [p_a, p_b]

        def xview(sc):
            return xcs[sc % 2][:].rearrange("p (f t) -> p f t", t=SC)

        with nc.Block() as b2:
            @b2.vector
            def _(v):
                v.wait_ge(S["s_mm"], NT1)
                v.wait_ge(S["s_sf"], 16 * NSP1)
                v.memset(carry[:], 0.0)
                v.drain()
                v.sem_inc(S["v_carry"], 1)

                def dve_dmx(H):
                    c, hl = H // 16, H % 16
                    ck = c % 2
                    st = H % 4
                    if H >= 4:
                        v.wait_ge(S["s_E"], H - 3)
                        v.wait_ge(S["s_peh"], H - 3)
                    v.tensor_scalar(dmx[:, st * 256:(st + 1) * 256], tri[:],
                                    tch[:, ck * 64 + 32 + hl:ck * 64 + 33 + hl],
                                    None, OP.mult)
                    v.tensor_scalar(Bddb[:, st * 64:(st + 1) * 64], Bt_sb[:],
                                    tch[:, ck * 64 + 16 + hl:ck * 64 + 17 + hl],
                                    None, OP.mult)
                    v.tensor_copy(prevb[:, st * 128:(st + 1) * 128],
                                  carry[:, hl * 128:(hl + 1) * 128])
                    v.drain()
                    v.sem_inc(S["v_dmx"], 1)

                def dve_st(H, xv):
                    c, hl = H // 16, H % 16
                    ck = c % 2
                    st = H % 4
                    v.wait_ge(S["a_exp"], H + 1)
                    v.scalar_tensor_tensor(
                        Stb[:, st * 128:(st + 1) * 128],
                        expe[:, st * 256:st * 256 + 128],
                        tch[:, ck * 64 + hl:ck * 64 + hl + 1],
                        Gm_sb[:, ck * 128:(ck + 1) * 128], OP.mult, OP.mult)
                    v.tensor_tensor(
                        Svb[:, st * 128:(st + 1) * 128],
                        xv[0:64, 17, (c % 4) * 128:(c % 4 + 1) * 128],
                        expe[0:64, st * 256 + 128:st * 256 + 256], OP.mult)
                    v.drain()
                    v.sem_inc(S["v_st2"], 1)

                def dve_yev(H, xv):
                    c, hl = H // 16, H % 16
                    ck = c % 2
                    st = H % 4
                    v.wait_ge(S["s_peh"], H + 1)
                    if hl == 0 and c >= 1:
                        v.wait_ge(S["a_ysq"], c)
                    v.scalar_tensor_tensor(
                        ybuf[:, hl * 128:(hl + 1) * 128],
                        xv[:, hl, (c % 4) * 128:(c % 4 + 1) * 128],
                        Db[:, hl:hl + 1], p_e[st][:, 256:384], OP.mult, OP.add)
                    v.scalar_tensor_tensor(
                        carry[:, hl * 128:(hl + 1) * 128],
                        carry[:, hl * 128:(hl + 1) * 128],
                        cdecb[0:64, ck * 16 + hl:ck * 16 + hl + 1],
                        p_e[st][0:64, 384:512], OP.mult, OP.add)
                    v.drain()
                    v.sem_inc(S["v_yev"], 1)

                for c in range(NC_CH):
                    sc, cl, ck = c // 4, c % 4, c % 2
                    if cl == 0:
                        v.wait_ge(S["s_xbc"], 48 * (sc + 1))
                        xb3 = xbcs[:].rearrange("p (f t) -> p f t", t=SC + 3)
                        cw3 = convw[:].rearrange("p (f k) -> p f k", k=5)
                        def cwb(k):
                            s_ = cw3[:, :, k:k + 1]
                            return bass.AP(s_.tensor, s_.offset,
                                           [s_.ap[0], s_.ap[1], [0, SC]])
                        a3 = accs[:].rearrange("p (f t) -> p f t", t=SC)
                        b3_ = cvt[:].rearrange("p (f t) -> p f t", t=SC)
                        if sc == 0:
                            v.memset(xb3[:, :, 0:3], 0)
                            v.drain()
                        v.tensor_tensor(a3, xb3[:, :, 0:SC], cwb(0), OP.mult)
                        v.drain()
                        for k in range(1, 4):
                            v.tensor_tensor(b3_, xb3[:, :, k:k + SC], cwb(k), OP.mult)
                            v.drain()
                            v.tensor_tensor(a3, a3, b3_, OP.add)
                            v.drain()
                        v.tensor_tensor(a3, a3, cwb(4), OP.add)
                        v.drain()
                        v.tensor_copy(xb3[:, :, 0:3], xb3[:, :, SC:SC + 3])
                        v.drain()
                        v.sem_inc(S["v_conv"], 1)
                    xv = xcs[sc % 2][:].rearrange("p (f t) -> p f t", t=SC)
                    v.drain()
                    v.wait_ge(S["s_tp"], c + 1)
                    if c >= 2:
                        v.wait_ge(S["s_peh"], 16 * (c - 1))
                    v.tensor_copy(xt_sb[:, ck * 2048:ck * 2048 + 1024], p_tA[:])
                    v.tensor_copy(xt_sb[:, ck * 2048 + 1024:ck * 2048 + 2048], p_tB[:])
                    v.tensor_copy(Bt_sb[:], p_tC[:, 0:64])
                    v.tensor_copy(tch[:, ck * 64:ck * 64 + 48], p_tD[:, 0:48])
                    v.tensor_copy(tch[0:1, ck * 64 + 48:ck * 64 + 64],
                                  p_tD[0:1, 48:64])
                    v.drain()
                    v.sem_inc(S["v_cdecb"], 1)
                    v.tensor_tensor(Gm_sb[:, ck * 128:(ck + 1) * 128],
                                    p_tD[:, 128:256], uin[:], OP.mult)
                    v.drain()
                    v.sem_inc(S["v_tpe"], 1)
                    v.wait_ge(S["s_cdmm"], c + 1)
                    v.tensor_copy(cdecb[:, ck * 16:(ck + 1) * 16], p_tD[:, 272:288])
                    v.drain()
                    for hl in range(16):
                        H = c * 16 + hl
                        dve_dmx(H)
                        if hl >= 1:
                            dve_st(H - 1, xv)
                        if hl >= 2:
                            dve_yev(H - 2, xv)
                    dve_st(c * 16 + 15, xv)
                    dve_yev(c * 16 + 14, xv)
                    dve_yev(c * 16 + 15, xv)
                    v.wait_ge(S["a_sz"], c + 1)
                    if c >= 4:
                        v.wait_ge(S["s_t1st"], 16 * (c // 4))
                    t1v = t1st[:].rearrange("p (f t) -> p f t", t=512)
                    v.tensor_tensor(t1v[:, :, cl * 128:(cl + 1) * 128],
                                    ybuf[:].rearrange("p (f t) -> p f t", t=128),
                                    zc[c % 2][:].rearrange("p (f t) -> p f t", t=128),
                                    OP.mult)
                    v.drain()
                    v.sem_inc(S["v_t1"], 1)
                    v.wait_ge(S["s_qmm"], c + 1)
                    if c >= 1:
                        v.wait_ge(S["s_ssqst"], 16 * c)
                    v.tensor_copy(ssqrow[:], p_tD[0:1, 384:512])
                    v.drain()
                    v.sem_inc(S["v_carry"], 1)

            @b2.scalar
            def _(a):
                for c in range(NC_CH):
                    sc, cl = c // 4, c % 4
                    if cl == 0:
                        a.wait_ge(S["v_conv"], sc + 1)
                        if sc >= 2:
                            a.wait_ge(S["v_yev"], 64 * (sc - 1))
                        a.activation(xcs[sc % 2][:], accs[:], AF.Silu)
                        a.drain()
                        a.sem_inc(S["a_xc"], 1)
                    a.wait_ge(S["s_zld"], 16 * (c + 1))
                    a.activation(zc[c % 2][:], zc[c % 2][:], AF.Silu)
                    a.drain()
                    a.sem_inc(S["a_sz"], 1)
                    for hl in range(16):
                        H = c * 16 + hl
                        st = H % 4
                        a.wait_ge(S["s_E"], H + 1)
                        if H >= 4:
                            a.wait_ge(S["v_st2"], H - 3)
                        a.activation(expe[:, st * 256:(st + 1) * 256],
                                     p_e[st][:, 0:256], AF.Exp)
                        a.drain()
                        a.sem_inc(S["a_exp"], 1)
                    a.wait_ge(S["v_yev"], 16 * (c + 1))
                    a.activation(ysq[:], ybuf[:], AF.Square)
                    a.drain()
                    a.sem_inc(S["a_ysq"], 1)

            @b2.tensor
            def _(t):
                t.wait_ge(S["v_p1b"], 3)
                t.wait_ge(S["s_ev"], NT1)

                def pe_e(H):
                    st = H % 4
                    t.wait_ge(S["v_dmx"], H + 1)
                    if H >= 4:
                        t.wait_ge(S["a_exp"], H - 3)
                        t.wait_ge(S["v_yev"], H - 3)
                    t.matmul(p_e[st][:, 0:128], dmx[:, st * 256:st * 256 + 128],
                             uin[:], start=True, stop=True)
                    t.matmul(p_e[st][:, 128:256],
                             dmx[:, st * 256 + 128:st * 256 + 256],
                             uin[:], start=True, stop=True).then_inc(S["s_E"], 1)

                def pe_ys(H):
                    c, hl = H // 16, H % 16
                    st = H % 4
                    ck = c % 2
                    t.wait_ge(S["v_st2"], H + 1)
                    t.matmul(p_e[st][:, 256:384],
                             xt_sb[:, ck * 2048 + hl * 128:ck * 2048 + (hl + 1) * 128],
                             Stb[:, st * 128:(st + 1) * 128], start=True, stop=False)
                    t.matmul(p_e[st][:, 256:384],
                             prevb[:, st * 128:(st + 1) * 128],
                             Svb[:, st * 128:(st + 1) * 128], start=False, stop=True)
                    t.matmul(p_e[st][0:64, 384:512],
                             Bddb[:, st * 64:(st + 1) * 64],
                             xt_sb[:, ck * 2048 + hl * 128:ck * 2048 + (hl + 1) * 128],
                             start=True, stop=True).then_inc(S["s_peh"], 1)

                for c in range(NC_CH):
                    sc, cl, ck = c // 4, c % 4, c % 2
                    xv = xcs[sc % 2][:].rearrange("p (f t) -> p f t", t=SC)
                    t.wait_ge(S["a_xc"], sc + 1)
                    t.wait_ge(S["v_carry"], c + 1)
                    if c >= 1:
                        t.wait_ge(S["v_tpe"], c)
                    for hh in range(16):
                        dstp = p_tA if hh < 8 else p_tB
                        t.transpose(dstp[:, (hh % 8) * 128:(hh % 8 + 1) * 128],
                                    xv[:, hh, cl * 128:(cl + 1) * 128], idb[:])
                    t.transpose(p_tC[:, 0:64], xv[0:64, 16, cl * 128:(cl + 1) * 128],
                                idb[0:64, 0:64])
                    t.transpose(p_tD[:, 0:16], dtq[:, c * 128:(c + 1) * 128],
                                idf[0:16, 0:16])
                    t.transpose(p_tD[:, 16:32], wtl[:, c * 128:(c + 1) * 128],
                                idf[0:16, 0:16])
                    t.transpose(p_tD[:, 32:48], dAq[:, c * 128:(c + 1) * 128],
                                idf[0:16, 0:16])
                    t.transpose(p_tD[0:1, 48:64], alast[:, c:c + 1], idf[0:16, 0:16])
                    t.matmul(p_tD[:, 128:256], xv[0:64, 16, cl * 128:(cl + 1) * 128],
                             xv[0:64, 17, cl * 128:(cl + 1) * 128],
                             start=True, stop=True).then_inc(S["s_tp"], 1)
                    t.wait_ge(S["v_cdecb"], c + 1)
                    t.matmul(p_tD[:, 272:288], orf[:],
                             tch[0:1, ck * 64 + 48:ck * 64 + 64],
                             start=True, stop=True).then_inc(S["s_cdmm"], 1)
                    for hl in range(16):
                        H = c * 16 + hl
                        pe_e(H)
                        if hl >= 1:
                            pe_ys(H - 1)
                    pe_ys(c * 16 + 15)
                    t.wait_ge(S["a_ysq"], c + 1)
                    for f in range(16):
                        mm = t.matmul(p_tD[0:1, 384:512], ocb[:],
                                      ysq[:, f * 128:(f + 1) * 128],
                                      start=(f == 0), stop=(f == 15))
                    mm.then_inc(S["s_qmm"], 1)

            @b2.sync
            def _(sy):
                with sy.register("t1offc") as rtc, sy.register("sqoff") as rq:
                    for c in range(NC_CH):
                        sc, cl = c // 4, c % 4
                        if cl == 0:
                            if sc == 0:
                                sy.wait_ge(S["s_mm"], NT1)
                                sy.wait_ge(S["s_sf"], 16 * NSP1)
                                sy.wait_ge(S["v_p1b"], 3)
                            else:
                                sy.wait_ge(S["v_conv"], sc)
                            if sc >= 2:
                                sy.wait_ge(S["v_yev"], 64 * (sc - 1))
                            if sc >= 1:
                                sy.wait_ge(S["s_xbc"], 48 * sc)
                            src = sp_xbc.ap()[:, :, 3 + sc * SC:3 + (sc + 1) * SC] \
                                .rearrange("f p t -> p f t")
                            dst = xbcs[:].rearrange("p (f t) -> p f t",
                                                    t=SC + 3)[:, 0:17, 3:SC + 3]
                            sy.dma_start(out=dst, in_=src).then_inc(S["s_xbc"], 16)
                            dst2 = xbcs[:].rearrange("p (f t) -> p f t",
                                                     t=SC + 3)[0:64, 17, 3:SC + 3]
                            sy.dma_start(
                                out=dst2,
                                in_=sp_xbc.ap()[16, 64:128, 3 + sc * SC:3 + (sc + 1) * SC]
                            ).then_inc(S["s_xbc"], 16)
                            dst3 = xbcs[:].rearrange("p (f t) -> p f t",
                                                     t=SC + 3)[64:128, 17, 3:SC + 3]
                            sy.dma_start(
                                out=dst3,
                                in_=sp_xbc.ap()[16, 0:64, 3 + sc * SC:3 + (sc + 1) * SC]
                            ).then_inc(S["s_xbc"], 16)
                        if c >= 2:
                            sy.wait_ge(S["v_t1"], c - 1)
                        if c >= 1:
                            sy.wait_ge(S["s_zld"], 16 * c)
                        src = sp_z.ap()[:, :, c * 128:(c + 1) * 128] \
                            .rearrange("f p t -> p f t")
                        dst = zc[c % 2][:].rearrange("p (f t) -> p f t", t=128)
                        sy.dma_start(out=dst, in_=src).then_inc(S["s_zld"], 16)
                        sy.wait_ge(S["v_carry"], c + 2)
                        if c >= 1:
                            sy.wait_ge(S["s_ssqst"], 16 * c)
                        sy.dma_start(out=sp_ssq.ap()[0:1, c * 128:(c + 1) * 128],
                                     in_=ssqrow[:]).then_inc(S["s_ssqst"], 16)
                    del rtc, rq

            @b2.gpsimd
            def _(g_):
                for i in range(16):
                    if i >= 1:
                        g_.wait_ge(S["s_pc"], 16 * i)
                    g_.dma_start(out=wo_bf.ap()[i],
                                 in_=woutT_h.ap()[i]).then_inc(S["s_pc"], 16)
                with g_.register("pt_g") as rp2, g_.register("t1o") as rt1, \
                     g_.register("t1oc") as rtc, g_.register("sqo") as rq:
                    g_.wait_ge(S["s_load"], 176)
                    g_.reg_load(rp2, pf[:])
                    g_.reg_alu(rt1, rp2, SLOT_T1, OP.mult)
                    g_.reg_alu(rq, rp2, T, OP.mult)
                    for sc in range(8):
                        g_.wait_ge(S["v_t1"], 4 * (sc + 1))
                        if sc >= 1:
                            g_.wait_ge(S["s_t1st"], 16 * sc)
                        g_.reg_alu(rtc, rt1, sc * 512, OP.add)
                        t1dst = bass.AP(shm_t1, rtc,
                                        [[T, 128], [128 * T, NF], [1, 512]])
                        src = t1st[:].rearrange("p (f t) -> p f t", t=512)
                        g_.dma_start(out=t1dst, in_=src).then_inc(S["s_t1st"], 16)
                    g_.wait_ge(S["s_ssqst"], 16 * NC_CH)
                    sqdst = bass.AP(shm_ssq, rq, [[T, 1], [1, T]])
                    g_.dma_start(out=sqdst, in_=sp_ssq.ap()).then_inc(S["s_ssq2"], 16)
                g_.wait_ge(S["s_t1st"], 16 * 8)
                g_.wait_ge(S["s_ssq2"], 16)
                with g_.register("p_g") as rp, g_.register("foff") as rf, \
                     g_.register("poff") as rpo, g_.register("cnt") as rc, \
                     g_.register("f0") as r0, g_.register("f1") as r1, \
                     g_.register("n0") as rn0, g_.register("n1") as rn1, \
                     g_.register("e0") as re0, g_.register("nd") as rnd:
                    g_.reg_load(rp, pf[:])
                    g_.reg_load(rn0, noncb[0:1, 0:1])
                    g_.reg_load(rn1, noncb[0:1, 1:2])
                    g_.reg_alu(rf, rp, 2, OP.mult)
                    g_.reg_alu(rpo, rp, -2, OP.mult)
                    g_.reg_alu(rpo, rpo, 2, OP.add)
                    fdst = bass.AP(shm_flag, rf, [[2, 1], [1, 2]])
                    g_.dma_start(out=fdst, in_=noncb[0:1, 0:2]).then_inc(S["g_flag"], 16)
                    g_.wait_ge(S["g_flag"], 16)
                    g_.reg_mov(rc, 16)
                    g_.reg_mov(rnd, 1)
                    fsrc = bass.AP(shm_flag, rpo, [[2, 1], [1, 2]])
                    with g_.While(rnd):
                        g_.reg_alu(rc, rc, 16, OP.add)
                        g_.dma_start(out=flagr[:], in_=fsrc).then_inc(S["g_flag"], 16)
                        g_.wait_ge(S["g_flag"], rc)
                        g_.reg_load(r0, flagr[:, 0:1])
                        g_.reg_load(r1, flagr[:, 1:2])
                        g_.reg_alu(re0, r0, rn0, OP.is_equal)
                        g_.reg_alu(rnd, r1, rn1, OP.is_equal)
                        g_.reg_alu(rnd, rnd, re0, OP.bitwise_and)
                        g_.reg_alu(rnd, rnd, 1, OP.bitwise_xor)
                    g_.sem_inc(S["g_xchg"], 1)

        esP2.close()
        esDT.close()

        if upto == 2:
            with nc.Block() as bex:
                @bex.sync
                def _(sy):
                    sy.wait_ge(S["g_xchg"], 1)
                    sy.dma_start(out=out_h.ap()[0:128, 0:128],
                                 in_=idf[:]).then_inc(S["s_out"], 16)
                    sy.wait_ge(S["s_out"], 16)
            return nc

        esP3 = ExitStack()
        EP3 = esP3.enter_context
        g_sb = EP3(nc.sbuf_tensor("g_sb", [128, 32 * 1024], BF16))
        wob = [EP3(nc.sbuf_tensor(f"wob{i}", [128, 32 * 128], BF16)) for i in range(2)]
        rstdb = EP3(nc.sbuf_tensor("rstdb", [128, 1024], F32))
        sqa = EP3(nc.sbuf_tensor("sqa", [1, TH], F32))
        sqb = EP3(nc.sbuf_tensor("sqb", [1, TH], F32))
        rows = EP3(nc.sbuf_tensor("rows", [1, 5 * 1024], F32))
        xtht = EP3(nc.sbuf_tensor("xtht", [128, 2 * 512], F32))
        ascr = EP3(nc.sbuf_tensor("ascr", [128, 2 * 512], F32))
        x1scr = EP3(nc.sbuf_tensor("x1scr", [128, 2 * 512], F32))
        rbt = EP3(nc.sbuf_tensor("rbt", [128, 2 * 512], BF16))
        rqt = EP3(nc.sbuf_tensor("rqt", [128, 2 * 512], BF16))
        p_u0 = EP3(nc.psum_tensor("p_u0", [128, 512], F32))
        p_u1 = EP3(nc.psum_tensor("p_u1", [128, 512], F32))
        p_sm = EP3(nc.psum_tensor("p_sm", [1, 1024], F32))
        p_sq = EP3(nc.psum_tensor("p_sq", [1, 1024], F32))

        with nc.Block() as b3:
            @b3.sync
            def _(sy):
                if True:
                    fwc = xthc = r1c = x1c = 0

                    def oinc(sem, cnt):
                        if cnt >= 1:
                            sy.wait_ge(S[sem], 16 * cnt)
                        return cnt + 1
                    sy.wait_ge(S["s_pc"], 16 * 16)
                    sy.wait_ge(S["g_xchg"], 1)
                    fwc = oinc("s_fw", fwc)
                    sy.dma_start(out=wob[0][:],
                                 in_=wo_bf.ap()[0]).then_inc(S["s_fw"], 16)
                    fwc = oinc("s_fw", fwc)
                    sy.dma_start(out=wob[1][:],
                                 in_=wo_bf.ap()[1]).then_inc(S["s_fw"], 16)
                    for tb3 in range(2):
                        for ot in range(16):
                            gi = tb3 * 16 + ot
                            for tq in range(2):
                                K = tb3 * 32 + ot * 2 + tq
                                if K >= 2:
                                    sy.wait_ge(S["v_att"], K - 1)
                                xthc = oinc("s_xth", xthc)
                                sy.dma_start(
                                    out=xtht[:, (K % 2) * 512:(K % 2 + 1) * 512],
                                    in_=xTh_h.ap()[ot * 128:(ot + 1) * 128,
                                                   tb3 * 1024 + tq * 512:
                                                   tb3 * 1024 + (tq + 1) * 512]
                                ).then_inc(S["s_xth"], 16)
                                sy.wait_ge(S["v_att"], K + 1)
                                r1c = oinc("s_r1", r1c)
                                sy.dma_start(
                                    out=sp_r1.ap()[ot, :,
                                                   tb3 * 1024 + tq * 512:
                                                   tb3 * 1024 + (tq + 1) * 512],
                                    in_=ascr[:, (K % 2) * 512:(K % 2 + 1) * 512]
                                ).then_inc(S["s_r1"], 16)
                            gnext = gi + 2
                            if gnext <= 31:
                                sy.wait_ge(S["s_gu"], 2 * (gi + 1))
                                fwc = oinc("s_fw", fwc)
                                sy.dma_start(out=wob[gnext % 2][:],
                                             in_=wo_bf.ap()[gnext - 16 if gnext >= 16 else gnext]
                                             ).then_inc(S["s_fw"], 16)
                        sy.wait_ge(S["s_r1"], 16 * 32 * (tb3 + 1))
                        for tq in range(2):
                            for ot in range(16):
                                K2 = tb3 * 32 + tq * 16 + ot
                                if K2 >= 2:
                                    sy.wait_ge(S["v_x1"], K2 - 1)
                                xthc = oinc("s_xth", xthc)
                                sy.dma_start(
                                    out=xtht[:, (K2 % 2) * 512:(K2 % 2 + 1) * 512],
                                    in_=sp_r1.ap()[ot, :,
                                                   tb3 * 1024 + tq * 512:
                                                   tb3 * 1024 + (tq + 1) * 512]
                                ).then_inc(S["s_xth"], 16)
                                sy.wait_ge(S["v_x1"], K2 + 1)
                                x1c = oinc("s_x1t", x1c)
                                sy.dma_start(
                                    out=sp_x1.ap()[ot, :,
                                                   tb3 * 1024 + tq * 512:
                                                   tb3 * 1024 + (tq + 1) * 512],
                                    in_=x1scr[:, (K2 % 2) * 512:(K2 % 2 + 1) * 512]
                                ).then_inc(S["s_x1t"], 16)

            @b3.gpsimd
            def _(g_):
                g_.wait_ge(S["g_xchg"], 1)
                with g_.register("p3p") as rp, g_.register("r1_") as r1, \
                     g_.register("r2_") as r2, g_.register("r3_") as r3:
                    g_.reg_load(rp, pf[:])
                    g_.reg_alu(r1, rp, TH, OP.mult)
                    sA = bass.AP(shm_ssq, r1, [[T, 1], [1, TH]])
                    g_.dma_start(out=sqa[:], in_=sA).then_inc(S["s_g"], 16)
                    g_.reg_alu(r3, r1, T, OP.add)
                    sB = bass.AP(shm_ssq, r3, [[T, 1], [1, TH]])
                    g_.wait_ge(S["s_g"], 16)
                    g_.dma_start(out=sqb[:], in_=sB).then_inc(S["s_g"], 16)
                    for tb3 in range(2):
                        if tb3 == 1:
                            g_.wait_ge(S["s_gu"], 32)
                        g_.reg_alu(r2, rp, SLOT_T1 + TH, OP.mult)
                        gdst = g_sb[:].rearrange("p (f t) -> p f t", t=1024)
                        g_.reg_alu(r3, r2, tb3 * 1024, OP.add)
                        srcA = bass.AP(shm_t1, r3,
                                       [[T, 128], [128 * T, NF], [1, 1024]])
                        g_.wait_ge(S["s_g"], 32 + tb3 * 32)
                        g_.dma_start(out=gdst[:, 0:16, :],
                                     in_=srcA).then_inc(S["s_g"], 16)
                        g_.reg_alu(r1, rp, TH - SLOT_T1, OP.mult)
                        g_.reg_alu(r1, r1, SLOT_T1, OP.add)
                        g_.reg_alu(r1, r1, tb3 * 1024, OP.add)
                        srcB = bass.AP(shm_t1, r1,
                                       [[T, 128], [128 * T, NF], [1, 1024]])
                        g_.wait_ge(S["s_g"], 48 + tb3 * 32)
                        g_.dma_start(out=gdst[:, 16:32, :],
                                     in_=srcB).then_inc(S["s_g"], 16)

            @b3.vector
            def _(v):
                v.wait_ge(S["s_g"], 32)
                v.tensor_add(sqa[:], sqa[:], sqb[:])
                v.drain()
                v.tensor_scalar(sqa[:], sqa[:], 1.0 / 4096.0, EPS_RMS,
                                OP.mult, OP.add)
                v.drain()
                v.sem_inc(S["v_rstd"], 1)
                v.wait_ge(S["a_rstd"], 1)
                v.reciprocal(sqa[:], sqa[:])
                v.drain()
                v.sem_inc(S["v_rstd"], 1)
                for tb3 in range(2):
                    for tq in range(2):
                        v.wait_ge(S["s_brc"], tb3 * 6 + tq + 1)
                        v.tensor_copy(rstdb[:, tq * 512:(tq + 1) * 512],
                                      p_u0[:] if tq == 0 else p_u1[:])
                    v.drain()
                    v.sem_inc(S["v_rstd"], 1)
                    for ot in range(16):
                        for tq in range(2):
                            K = tb3 * 32 + ot * 2 + tq
                            pp = p_a if K % 2 == 0 else p_b
                            v.wait_ge(S["s_gu"], K + 1)
                            v.wait_ge(S["s_xth"], 16 * (tb3 * 64 + ot * 2 + tq + 1))
                            if K >= 2:
                                v.wait_ge(S["s_r1"], 16 * (K - 1))
                            v.tensor_tensor(ascr[:, (K % 2) * 512:(K % 2 + 1) * 512],
                                            pp[:, 0:512],
                                            rstdb[:, tq * 512:(tq + 1) * 512],
                                            OP.mult)
                            v.drain()
                            v.tensor_add(ascr[:, (K % 2) * 512:(K % 2 + 1) * 512],
                                         ascr[:, (K % 2) * 512:(K % 2 + 1) * 512],
                                         xtht[:, (K % 2) * 512:(K % 2 + 1) * 512])
                            v.drain()
                            if K >= 2:
                                v.wait_ge(S["s_qmv"], 2 * (K - 1))
                            v.tensor_copy(rbt[:, (K % 2) * 512:(K % 2 + 1) * 512],
                                          ascr[:, (K % 2) * 512:(K % 2 + 1) * 512])
                            v.drain()
                            v.sem_inc(S["v_att"], 1)
                    v.wait_ge(S["s_qmv"], 2 * 32 * (tb3 + 1))
                    v.tensor_scalar_mul(rows[0:1, 0:1024], p_sm[0:1, :], 1.0 / DM)
                    v.tensor_scalar_mul(rows[0:1, 1024:2048], p_sq[0:1, :], 1.0 / DM)
                    v.drain()
                    v.tensor_mul(rows[0:1, 4096:5120], rows[0:1, 0:1024],
                                 rows[0:1, 0:1024])
                    v.drain()
                    v.tensor_sub(rows[0:1, 1024:2048], rows[0:1, 1024:2048],
                                 rows[0:1, 4096:5120])
                    v.drain()
                    v.tensor_scalar(rows[0:1, 1024:2048], rows[0:1, 1024:2048],
                                    1.0, EPS_LN, OP.mult, OP.add)
                    v.drain()
                    v.sem_inc(S["v_row"], 1)
                    v.wait_ge(S["a_row"], tb3 + 1)
                    v.reciprocal(rows[0:1, 2048:3072], rows[0:1, 1024:2048])
                    v.drain()
                    v.tensor_mul(rows[0:1, 3072:4096], rows[0:1, 0:1024],
                                 rows[0:1, 2048:3072])
                    v.drain()
                    v.tensor_scalar_mul(rows[0:1, 3072:4096],
                                        rows[0:1, 3072:4096], -1.0)
                    v.drain()
                    v.sem_inc(S["v_row"], 1)
                    for tq in range(2):
                        for ot in range(16):
                            K2 = tb3 * 32 + tq * 16 + ot
                            v.wait_ge(S["s_brc"], tb3 * 6 + 2 + 2 * (tq + 1))
                            v.wait_ge(S["s_xth"],
                                      16 * (tb3 * 64 + 32 + tq * 16 + ot + 1))
                            if K2 >= 2:
                                v.wait_ge(S["s_x1t"], 16 * (K2 - 1))
                            v.tensor_tensor(
                                x1scr[:, (K2 % 2) * 512:(K2 % 2 + 1) * 512],
                                xtht[:, (K2 % 2) * 512:(K2 % 2 + 1) * 512],
                                p_u0[:], OP.mult)
                            v.drain()
                            v.tensor_tensor(
                                x1scr[:, (K2 % 2) * 512:(K2 % 2 + 1) * 512],
                                x1scr[:, (K2 % 2) * 512:(K2 % 2 + 1) * 512],
                                p_u1[:], OP.add)
                            v.drain()
                            v.sem_inc(S["v_x1"], 1)

            @b3.scalar
            def _(a):
                a.wait_ge(S["v_rstd"], 1)
                a.activation(sqa[:], sqa[:], AF.Sqrt)
                a.drain()
                a.sem_inc(S["a_rstd"], 1)
                for tb3 in range(2):
                    for ot in range(16):
                        for tq in range(2):
                            K = tb3 * 32 + ot * 2 + tq
                            a.wait_ge(S["v_att"], K + 1)
                            if K >= 2:
                                a.wait_ge(S["s_qmv"], 2 * (K - 1))
                            a.activation(rqt[:, (K % 2) * 512:(K % 2 + 1) * 512],
                                         rbt[:, (K % 2) * 512:(K % 2 + 1) * 512],
                                         AF.Square)
                            a.drain()
                            a.sem_inc(S["a_sq2"], 1)
                    a.wait_ge(S["v_row"], 2 * tb3 + 1)
                    a.activation(rows[0:1, 1024:2048], rows[0:1, 1024:2048], AF.Sqrt)
                    a.drain()
                    a.sem_inc(S["a_row"], 1)

            @b3.tensor
            def _(t):
                gv = g_sb[:].rearrange("p (f t) -> p f t", t=1024)
                t.wait_ge(S["v_yev"], 512)
                t.wait_ge(S["a_exp"], 512)
                for tb3 in range(2):
                    t.wait_ge(S["v_rstd"], 2)
                    if tb3 == 1:
                        t.wait_ge(S["v_x1"], 32)
                    t.matmul(p_u0[:], orf[:],
                             sqa[0:1, tb3 * 1024:tb3 * 1024 + 512],
                             start=True, stop=True).then_inc(S["s_brc"], 1)
                    t.matmul(p_u1[:], orf[:],
                             sqa[0:1, tb3 * 1024 + 512:tb3 * 1024 + 1024],
                             start=True, stop=True).then_inc(S["s_brc"], 1)
                    t.wait_ge(S["s_g"], 32 + 32 * (tb3 + 1))
                    for ot in range(16):
                        gi = tb3 * 16 + ot
                        t.wait_ge(S["s_fw"], 16 * (gi + 1))
                        wv = wob[gi % 2][:].rearrange("p (ic o) -> p ic o", o=128)
                        for tq in range(2):
                            K = tb3 * 32 + ot * 2 + tq
                            pp = p_a if K % 2 == 0 else p_b
                            if K >= 2:
                                t.wait_ge(S["v_att"], K - 1)
                            for ic in range(32):
                                mm = t.matmul(pp[:, 0:512], wv[:, ic, :],
                                              gv[:, ic, tq * 512:(tq + 1) * 512],
                                              start=(ic == 0), stop=(ic == 31))
                            mm.then_inc(S["s_gu"], 1)
                            t.wait_ge(S["a_sq2"], K + 1)
                            t.matmul(p_sm[0:1, tq * 512:(tq + 1) * 512], ocb[:],
                                     rbt[:, (K % 2) * 512:(K % 2 + 1) * 512],
                                     start=(ot == 0), stop=(ot == 15))
                            t.matmul(p_sq[0:1, tq * 512:(tq + 1) * 512], ocb[:],
                                     rqt[:, (K % 2) * 512:(K % 2 + 1) * 512],
                                     start=(ot == 0),
                                     stop=(ot == 15)).then_inc(S["s_qmv"], 2)
                    t.wait_ge(S["v_row"], 2 * tb3 + 2)
                    for tq in range(2):
                        if tb3 * 2 + tq >= 1:
                            t.wait_ge(S["v_x1"], 16 * (tb3 * 2 + tq))
                        t.matmul(p_u0[:], orf[:],
                                 rows[0:1, 2048 + tq * 512:2048 + (tq + 1) * 512],
                                 start=True, stop=True)
                        t.matmul(p_u1[:], orf[:],
                                 rows[0:1, 3072 + tq * 512:3072 + (tq + 1) * 512],
                                 start=True, stop=True).then_inc(S["s_brc"], 2)
        esP3.close()

        if upto == 3:
            with nc.Block() as bex:
                @bex.sync
                def _(sy):
                    sy.wait_ge(S["s_x1t"], 16 * 64)
                    sy.dma_start(out=out_h.ap()[0:128, 0:128],
                                 in_=idf[:]).then_inc(S["s_out"], 16)
                    sy.wait_ge(S["s_out"], 16)
            return nc

        esP5 = ExitStack()
        EP5 = esP5.enter_context
        h_sb = EP5(nc.sbuf_tensor("h_sb", [128, 64 * 512], BF16))
        x1b = EP5(nc.sbuf_tensor("x1b", [128, 16 * 512], BF16))
        wp = [EP5(nc.sbuf_tensor(f"wp{i}", [128, 16 * 256], BF16)) for i in range(2)]
        wd = [EP5(nc.sbuf_tensor(f"wd{i}", [128, 64 * 128], BF16)) for i in range(2)]
        rows5 = EP5(nc.sbuf_tensor("rows5", [1, 5 * 512], F32))
        xt5 = EP5(nc.sbuf_tensor("xt5", [128, 2 * 512], F32))
        r2scr = EP5(nc.sbuf_tensor("r2scr", [128, 2 * 512], F32))
        o5scr = EP5(nc.sbuf_tensor("o5scr", [128, 2 * 512], F32))
        rb5 = EP5(nc.sbuf_tensor("rb5", [128, 2 * 512], BF16))
        rq5 = EP5(nc.sbuf_tensor("rq5", [128, 2 * 512], BF16))
        sg5 = EP5(nc.sbuf_tensor("sg5", [128, 2 * 512], BF16))
        p_u0b = EP5(nc.psum_tensor("p_u0b", [128, 512], F32))
        p_u1b = EP5(nc.psum_tensor("p_u1b", [128, 512], F32))
        p_sm5 = EP5(nc.psum_tensor("p_sm5", [1, 512], F32))
        p_sq5 = EP5(nc.psum_tensor("p_sq5", [1, 512], F32))

        NFW0 = 32 * 16
        XTH0 = 128
        for tb in range(4):
            with nc.Block() as b5a:
                @b5a.gpsimd
                def _(g_, tb=tb):
                    if tb == 0:
                        g_.wait_ge(S["s_x1t"], 16 * 64)
                    else:
                        g_.wait_ge(S["s_gu"], 64 + 64 * tb)
                        g_.wait_ge(S["g_x1b"], 16 * tb)
                    src = sp_x1.ap()[:, :, tb * 512:(tb + 1) * 512] \
                        .rearrange("f p t -> p f t")
                    dst = x1b[:].rearrange("p (f t) -> p f t", t=512)
                    g_.dma_start(out=dst, in_=src).then_inc(S["g_x1b"], 16)
                    for ht in range(64):
                        hi = tb * 64 + ht
                        if hi >= 2:
                            g_.wait_ge(S["s_gu"], 64 + hi - 1)
                        dstw = wp[hi % 2][:].rearrange("p (ic o) -> p ic o", o=256)
                        sg = "g_wp" if hi % 2 == 0 else "g_wp2"
                        su = "g_wu" if hi % 2 == 0 else "g_wu2"
                        if hi >= 2:
                            g_.wait_ge(S[sg], 16 * (hi // 2))
                        g_.dma_start(
                            out=dstw[:, :, 0:128],
                            in_=wgT_h.ap()[ht].rearrange("p (ic o) -> p ic o", o=128)
                        ).then_inc(S[sg], 16)
                        if hi >= 2:
                            g_.wait_ge(S[su], 16 * (hi // 2))
                        g_.dma_start(
                            out=dstw[:, :, 128:256],
                            in_=wuT_h.ap()[ht].rearrange("p (ic o) -> p ic o", o=128)
                        ).then_inc(S[su], 16)

                @b5a.tensor
                def _(t, tb=tb):
                    xv5 = x1b[:].rearrange("p (f t) -> p f t", t=512)
                    t.wait_ge(S["g_x1b"], 16 * (tb + 1))
                    if tb >= 1:
                        t.wait_ge(S["v_out"], 16 * tb)
                        t.wait_ge(S["v_r2"], 16 * tb)
                    for ht in range(64):
                        hi = tb * 64 + ht
                        t.wait_ge(S["g_wp" if hi % 2 == 0 else "g_wp2"],
                                  16 * (hi // 2 + 1))
                        t.wait_ge(S["g_wu" if hi % 2 == 0 else "g_wu2"],
                                  16 * (hi // 2 + 1))
                        wv = wp[hi % 2][:].rearrange("p (ic o) -> p ic o", o=256)
                        pg = p_a if ht % 2 == 0 else p_b
                        pu = p_u0b if ht % 2 == 0 else p_u1b
                        if ht >= 2:
                            t.wait_ge(S["v_h"], 64 * tb + ht - 1)
                        for ic in range(16):
                            t.matmul(pg[:, 0:512], wv[:, ic, 0:128],
                                     xv5[:, ic, :], start=(ic == 0), stop=(ic == 15))
                        for ic in range(16):
                            mm = t.matmul(pu[:, 0:512], wv[:, ic, 128:256],
                                          xv5[:, ic, :], start=(ic == 0),
                                          stop=(ic == 15))
                        mm.then_inc(S["s_gu"], 1)

                @b5a.scalar
                def _(a, tb=tb):
                    for ht in range(64):
                        hi = tb * 64 + ht
                        a.wait_ge(S["s_gu"], 64 + hi + 1)
                        if ht >= 2:
                            a.wait_ge(S["v_h"], 64 * tb + ht - 1)
                        pg = p_a if ht % 2 == 0 else p_b
                        a.activation(sg5[:, (ht % 2) * 512:(ht % 2 + 1) * 512],
                                     pg[:, 0:512], AF.Silu)
                        a.drain()
                        a.sem_inc(S["a_sg"], 1)

                @b5a.vector
                def _(v, tb=tb):
                    hv = h_sb[:].rearrange("p (f t) -> p f t", t=512)
                    for ht in range(64):
                        hi = tb * 64 + ht
                        v.wait_ge(S["a_sg"], hi + 1)
                        if tb >= 1 and ht == 0:
                            v.wait_ge(S["s_dmm"], 16 * tb)
                        pu = p_u0b if ht % 2 == 0 else p_u1b
                        v.tensor_tensor(hv[:, ht, :],
                                        sg5[:, (ht % 2) * 512:(ht % 2 + 1) * 512],
                                        pu[:, 0:512], OP.mult)
                        v.drain()
                        v.sem_inc(S["v_h"], 1)

            with nc.Block() as b5b:
                @b5b.gpsimd
                def _(g_, tb=tb):
                    for ot in range(16):
                        di = tb * 16 + ot
                        if di >= 2:
                            g_.wait_ge(S["s_dmm"], di - 1)
                        if di >= 1:
                            g_.wait_ge(S["s_dw5"], 16 * di)
                        g_.dma_start(
                            out=wd[di % 2][:],
                            in_=wdT_h.ap()[ot]).then_inc(S["s_dw5"], 16)

                @b5b.sync
                def _(sy, tb=tb):
                    if tb == 0:
                        sy.wait_ge(S["s_x1t"], 16 * 64)
                    for ot in range(16):
                        K = tb * 16 + ot
                        if K >= 2:
                            sy.wait_ge(S["v_r2"], K - 1)
                        sy.wait_ge(S["s_xth"], 16 * (XTH0 + 32 * tb + ot))
                        sy.dma_start(
                            out=xt5[:, (K % 2) * 512:(K % 2 + 1) * 512],
                            in_=sp_x1.ap()[ot, :, tb * 512:(tb + 1) * 512]
                        ).then_inc(S["s_xth"], 16)
                        sy.wait_ge(S["v_r2"], K + 1)
                        if K >= 1:
                            sy.wait_ge(S["s_r2t"], 16 * K)
                        sy.dma_start(
                            out=sp_r2.ap()[ot, :, tb * 512:(tb + 1) * 512],
                            in_=r2scr[:, (K % 2) * 512:(K % 2 + 1) * 512]
                        ).then_inc(S["s_r2t"], 16)
                    sy.wait_ge(S["s_r2t"], 16 * 16 * (tb + 1))
                    for ot in range(16):
                        K2 = tb * 16 + ot
                        if K2 >= 2:
                            sy.wait_ge(S["v_out"], K2 - 1)
                        sy.wait_ge(S["s_xth"], 16 * (XTH0 + 32 * tb + 16 + ot))
                        sy.dma_start(
                            out=xt5[:, (K2 % 2) * 512:(K2 % 2 + 1) * 512],
                            in_=sp_r2.ap()[ot, :, tb * 512:(tb + 1) * 512]
                        ).then_inc(S["s_xth"], 16)
                        sy.wait_ge(S["v_out"], K2 + 1)
                        if K2 >= 1:
                            sy.wait_ge(S["s_out"], 16 * K2)
                        sy.dma_start(
                            out=out_h.ap()[ot * 128:(ot + 1) * 128,
                                           tb * 512:(tb + 1) * 512],
                            in_=o5scr[:, (K2 % 2) * 512:(K2 % 2 + 1) * 512]
                        ).then_inc(S["s_out"], 16)
                    if tb == 3:
                        sy.wait_ge(S["s_out"], 16 * 64)

                @b5b.tensor
                def _(t, tb=tb):
                    hv = h_sb[:].rearrange("p (f t) -> p f t", t=512)
                    t.wait_ge(S["v_h"], 64 * (tb + 1))
                    for ot in range(16):
                        di = tb * 16 + ot
                        K = di
                        t.wait_ge(S["s_dw5"], 16 * (di + 1))
                        wv = wd[di % 2][:].rearrange("p (ic o) -> p ic o", o=128)
                        pp = p_a if K % 2 == 0 else p_b
                        if K >= 2:
                            t.wait_ge(S["v_r2"], K - 1)
                        for ic in range(64):
                            mm = t.matmul(pp[:, 0:512], wv[:, ic, :], hv[:, ic, :],
                                          start=(ic == 0), stop=(ic == 63))
                        mm.then_inc(S["s_dmm"], 1)
                        t.wait_ge(S["a_sq3"], K + 1)
                        t.matmul(p_sm5[0:1, :], ocb[:],
                                 rb5[:, (K % 2) * 512:(K % 2 + 1) * 512],
                                 start=(ot == 0), stop=(ot == 15))
                        t.matmul(p_sq5[0:1, :], ocb[:],
                                 rq5[:, (K % 2) * 512:(K % 2 + 1) * 512],
                                 start=(ot == 0),
                                 stop=(ot == 15)).then_inc(S["s_qm3"], 1)
                    t.wait_ge(S["v_row2"], 2 * tb + 2)
                    if tb >= 1:
                        t.wait_ge(S["v_out"], 16 * tb)
                    t.matmul(p_u0b[:], orf[:], rows5[0:1, 1024:1536],
                             start=True, stop=True)
                    t.matmul(p_u1b[:], orf[:], rows5[0:1, 1536:2048],
                             start=True, stop=True).then_inc(S["s_br2"], 2)

                @b5b.scalar
                def _(a, tb=tb):
                    for ot in range(16):
                        K = tb * 16 + ot
                        a.wait_ge(S["v_r2"], K + 1)
                        if K >= 2:
                            a.wait_ge(S["s_qm3"], K - 1)
                        a.activation(rq5[:, (K % 2) * 512:(K % 2 + 1) * 512],
                                     rb5[:, (K % 2) * 512:(K % 2 + 1) * 512],
                                     AF.Square)
                        a.drain()
                        a.sem_inc(S["a_sq3"], 1)
                    a.wait_ge(S["v_row2"], 2 * tb + 1)
                    a.activation(rows5[0:1, 512:1024], rows5[0:1, 512:1024], AF.Sqrt)
                    a.drain()
                    a.sem_inc(S["a_row2"], 1)

                @b5b.vector
                def _(v, tb=tb):
                    for ot in range(16):
                        K = tb * 16 + ot
                        v.wait_ge(S["s_dmm"], K + 1)
                        v.wait_ge(S["s_xth"], 16 * (XTH0 + tb * 32 + ot + 1))
                        pp = p_a if K % 2 == 0 else p_b
                        if K >= 2:
                            v.wait_ge(S["s_r2t"], 16 * (K - 1))
                        v.tensor_add(r2scr[:, (K % 2) * 512:(K % 2 + 1) * 512],
                                     pp[:, 0:512],
                                     xt5[:, (K % 2) * 512:(K % 2 + 1) * 512])
                        v.drain()
                        if K >= 2:
                            v.wait_ge(S["s_qm3"], K - 1)
                        v.tensor_copy(rb5[:, (K % 2) * 512:(K % 2 + 1) * 512],
                                      r2scr[:, (K % 2) * 512:(K % 2 + 1) * 512])
                        v.drain()
                        v.sem_inc(S["v_r2"], 1)
                    v.wait_ge(S["s_qm3"], 16 * (tb + 1))
                    v.tensor_scalar_mul(rows5[0:1, 0:512], p_sm5[0:1, :], 1.0 / DM)
                    v.tensor_scalar_mul(rows5[0:1, 512:1024], p_sq5[0:1, :], 1.0 / DM)
                    v.drain()
                    v.tensor_mul(rows5[0:1, 2048:2560], rows5[0:1, 0:512],
                                 rows5[0:1, 0:512])
                    v.drain()
                    v.tensor_sub(rows5[0:1, 512:1024], rows5[0:1, 512:1024],
                                 rows5[0:1, 2048:2560])
                    v.drain()
                    v.tensor_scalar(rows5[0:1, 512:1024], rows5[0:1, 512:1024],
                                    1.0, EPS_LN, OP.mult, OP.add)
                    v.drain()
                    v.sem_inc(S["v_row2"], 1)
                    v.wait_ge(S["a_row2"], tb + 1)
                    v.reciprocal(rows5[0:1, 1024:1536], rows5[0:1, 512:1024])
                    v.drain()
                    v.tensor_mul(rows5[0:1, 1536:2048], rows5[0:1, 0:512],
                                 rows5[0:1, 1024:1536])
                    v.drain()
                    v.tensor_scalar_mul(rows5[0:1, 1536:2048],
                                        rows5[0:1, 1536:2048], -1.0)
                    v.drain()
                    v.sem_inc(S["v_row2"], 1)
                    for ot in range(16):
                        K2 = tb * 16 + ot
                        v.wait_ge(S["s_br2"], 2 * (tb + 1))
                        v.wait_ge(S["s_xth"], 16 * (XTH0 + tb * 32 + 16 + ot + 1))
                        if K2 >= 2:
                            v.wait_ge(S["s_out"], 16 * (K2 - 1))
                        v.tensor_tensor(
                            o5scr[:, (K2 % 2) * 512:(K2 % 2 + 1) * 512],
                            xt5[:, (K2 % 2) * 512:(K2 % 2 + 1) * 512],
                            p_u0b[:], OP.mult)
                        v.drain()
                        v.tensor_tensor(
                            o5scr[:, (K2 % 2) * 512:(K2 % 2 + 1) * 512],
                            o5scr[:, (K2 % 2) * 512:(K2 % 2 + 1) * 512],
                            p_u1b[:], OP.add)
                        v.drain()
                        v.sem_inc(S["v_out"], 1)
        esP5.close()
    return nc



def _prep_inputs(inputs):
    x = np.asarray(inputs["L0_attn_x"], np.float32)
    W_in = np.asarray(inputs["L0_attn_W_in"], np.float32)
    conv_w = np.asarray(inputs["L0_attn_conv_w"], np.float32)
    conv_b = np.asarray(inputs["L0_attn_conv_b"], np.float32)
    A_log = np.asarray(inputs["L0_attn_A_log"], np.float32)
    dt_bias = np.asarray(inputs["L0_attn_dt_bias"], np.float32)
    Dv = np.asarray(inputs["L0_attn_D"], np.float32)
    W_out = np.asarray(inputs["L0_attn_W_out"], np.float32)
    w_gate = np.asarray(inputs["L0_ffn_w_gate"], np.float32)
    w_up = np.asarray(inputs["L0_ffn_w_up"], np.float32)
    w_down = np.asarray(inputs["L0_ffn_w_down"], np.float32)

    def swz(WT, n_out_tiles, K):
        O = WT.shape[1]
        a = WT.reshape(K // 128, 128, n_out_tiles, O // n_out_tiles)
        return np.ascontiguousarray(a.transpose(2, 1, 0, 3).reshape(
            n_out_tiles, 128, (K // 128) * (O // n_out_tiles)))

    wgT = swz(np.ascontiguousarray(w_gate.T), 64, DM)
    wuT = swz(np.ascontiguousarray(w_up.T), 64, DM)
    wdT = swz(np.ascontiguousarray(w_down.T), 16, DH)

    in_maps = []
    rng = np.random.default_rng()
    for s in range(4):
        nonce_pair = rng.integers(1, 2**31 - 9, (4, 2), dtype=np.int32)
        xs = np.ascontiguousarray(x[s].T)
        for p in range(2):
            z_rows = np.arange(p * 2048, (p + 1) * 2048)
            xs_rows = np.arange(4096 + p * 2048, 4096 + (p + 1) * 2048)
            bc_rows = np.arange(8192, 8320)
            dt_rows = np.arange(8320 + p * 16, 8320 + (p + 1) * 16)
            rows_sel = np.concatenate([z_rows, xs_rows, bc_rows, dt_rows])
            winT0 = np.zeros((DM, WIN_COLS), np.float32)
            winT0[:, :rows_sel.size] = W_in[rows_sel, :].T
            winT = winT0.reshape(16, 128, NWG, WG).transpose(2, 1, 0, 3) \
                .reshape(NWG, 128, 16 * WG)
            my = np.arange(p * 2048, (p + 1) * 2048)
            pa = np.arange((1 - p) * 2048, (2 - p) * 2048)
            woutT = swz(np.ascontiguousarray(W_out[:, np.concatenate([my, pa])].T),
                        16, 2 * DM)
            cf = np.concatenate([xs_rows - 4096, np.arange(4096, 4224)])
            cw = conv_w[cf]
            cb = conv_b[cf]
            convp = np.zeros((128, NCT, 5), np.float32)
            for ftile in range(17):
                convp[:, ftile, 0:4] = cw[ftile * 128:(ftile + 1) * 128]
                convp[:, ftile, 4] = cb[ftile * 128:(ftile + 1) * 128]
            convp[0:64, 17, 0:4] = cw[2112:2176]
            convp[0:64, 17, 4] = cb[2112:2176]
            hcol = np.stack([A_log[p * 16:(p + 1) * 16],
                             dt_bias[p * 16:(p + 1) * 16]], axis=1)
            hrow = Dv[p * 16:(p + 1) * 16][None, :]
            xth = np.ascontiguousarray(x[s, p * 2048:(p + 1) * 2048, :].T)
            in_maps.append({
                "xT": xs, "xTh": np.ascontiguousarray(xth),
                "winT": np.ascontiguousarray(winT),
                "woutT": woutT, "wgT": wgT, "wuT": wuT, "wdT": wdT,
                "convp": np.ascontiguousarray(convp.reshape(128, NCT * 5)),
                "hcol": np.ascontiguousarray(hcol),
                "hrow": np.ascontiguousarray(hrow),
                "par": np.array([[p]], np.int32),
                "nonce": nonce_pair,
            })
    return in_maps


def kernel(**inputs):
    in_maps = _prep_inputs(inputs)
    nc = build_nc()
    res = run_bass_kernel_spmd(nc, in_maps, core_ids=list(range(8)))
    out = np.zeros((4, T, DM), np.float32)
    for s in range(4):
        for p in range(2):
            o = res.results[2 * s + p]["outT"]
            out[s, p * 2048:(p + 1) * 2048, :] = o.T
    return out
